# version 01
# baseline (speedup 1.0000x reference)
import numpy as np
import ml_dtypes
from contextlib import ExitStack

import concourse.bass as bass
import concourse.mybir as mybir
from concourse.bass_utils import run_bass_kernel_spmd

F32 = mybir.dt.float32
BF16 = mybir.dt.bfloat16
I32 = mybir.dt.int32
AF = mybir.ActivationFunctionType
ALU = mybir.AluOpType
AX = mybir.AxisListType
BF = ml_dtypes.bfloat16

NCORE = 8
DM = 2048
TOK = 16384
TSL = TOK // NCORE
NST = 16
DTR = 64
SEQ = 8192
NH = 16
DH = 128
FF0 = 8192
FI = 5632


def build_A():
    nc = bass.Bass("TRN2", target_bir_lowering=False)
    xT = nc.dram_tensor("xT", [DM, TSL], F32, kind="ExternalInput").ap()
    wxT = nc.dram_tensor("wxT", [DM, 96], F32, kind="ExternalInput").ap()
    wdtT = nc.dram_tensor("wdtT", [DTR, DM], F32, kind="ExternalInput").ap()
    bdt = nc.dram_tensor("bdt", [128, 16], F32, kind="ExternalInput").ap()
    dtT = nc.dram_tensor("dtT", [DM, TSL], F32, kind="ExternalOutput").ap()
    bc = nc.dram_tensor("bc", [32, TSL], F32, kind="ExternalOutput").ap()

    NK = DM // 128
    NTC = TSL // 512

    with ExitStack() as ctx:
        e = ctx.enter_context
        xb = e(nc.sbuf_tensor("xb", [128, NK * TSL], BF16))
        wx_sb = e(nc.sbuf_tensor("wx_sb", [128, NK * 96], BF16))
        wdt_sb = e(nc.sbuf_tensor("wdt_sb", [DTR, DM], BF16))
        bdt_sb = e(nc.sbuf_tensor("bdt_sb", [128, 16], F32))
        xp_sb = e(nc.sbuf_tensor("xp_sb", [96, TSL], F32))
        dtx_b = e(nc.sbuf_tensor("dtx_b", [DTR, TSL], BF16))
        t_z = e(nc.sbuf_tensor("t_z", [128, 2 * 512], F32))
        t_a = e(nc.sbuf_tensor("t_a", [128, 512], F32))
        t_e = e(nc.sbuf_tensor("t_e", [128, 512], F32))
        t_l = e(nc.sbuf_tensor("t_l", [128, 2 * 512], F32))
        t_r = e(nc.sbuf_tensor("t_r", [128, 512], F32))
        dt_out = e(nc.sbuf_tensor("dt_out", [128, 2 * 512], F32))
        one_c = e(nc.sbuf_tensor("one_c", [128, 1], F32))
        p_xp = e(nc.psum_tensor("p_xp", [96, 512], F32))
        p_dt = e(nc.psum_tensor("p_dt", [128, 2 * 512], F32))
        ld = e(nc.semaphore("ld"))
        lds = e(nc.semaphore("lds"))
        s_xp = e(nc.semaphore("s_xp"))
        s_dt = e(nc.semaphore("s_dt"))
        s_dtx = e(nc.semaphore("s_dtx"))
        s_spZ = e(nc.semaphore("s_spZ"))
        s_spL = e(nc.semaphore("s_spL"))
        s_vr = e(nc.semaphore("s_vr"))
        s_out = e(nc.semaphore("s_out"))
        stp = [e(nc.semaphore("st0")), e(nc.semaphore("st1"))]
        blk = e(nc.Block())

        NLD = 16 + 16 + 1 + 1 + 1

        @blk.gpsimd
        def _(g):
            for k in range(NK):
                g.dma_start(out=xb[:, k * TSL:(k + 1) * TSL],
                            in_=xT[k * 128:(k + 1) * 128, :]).then_inc(ld, 16)
            for k in range(NK):
                g.dma_start(out=wx_sb[:, k * 96:(k + 1) * 96],
                            in_=wxT[k * 128:(k + 1) * 128, :]).then_inc(ld, 16)
            g.dma_start(out=wdt_sb[:], in_=wdtT[:]).then_inc(ld, 16)

        @blk.sync
        def _(sy):
            sy.dma_start(out=bdt_sb[:], in_=bdt[:]).then_inc(lds, 16)

        LD_ALL = 16 * (NK + NK + 1)

        @blk.tensor
        def _(t):
            t.wait_ge(ld, LD_ALL)
            g = 0
            for tc in range(NTC):
                cs = slice(tc * 512, (tc + 1) * 512)
                if tc > 0:
                    t.wait_ge(s_dtx, tc)
                for k in range(NK):
                    mm = t.matmul(p_xp[:, :], wx_sb[:, k * 96:(k + 1) * 96],
                                  xb[:, k * TSL + tc * 512: k * TSL + tc * 512 + 512],
                                  start=(k == 0), stop=(k == NK - 1))
                mm.then_inc(s_xp, 1)
                t.wait_ge(s_dtx, tc + 1)
                for k in range(NK):
                    if g >= 2:
                        t.wait_ge(s_spZ, g - 1)
                    b = g % 2
                    t.matmul(p_dt[:, b * 512:(b + 1) * 512],
                             wdt_sb[:, k * 128:(k + 1) * 128], dtx_b[:, cs],
                             start=True, stop=True).then_inc(s_dt, 1)
                    g += 1

        @blk.scalar
        def _(a):
            a.wait_ge(lds, 16)
            g = 0
            for tc in range(NTC):
                for k in range(NK):
                    b = g % 2
                    zsl = slice(b * 512, (b + 1) * 512)
                    a.wait_ge(s_dt, g + 1)
                    if g >= 2:
                        a.wait_ge(s_vr, g - 1)
                    a.activation(t_z[:, zsl], p_dt[:, zsl], AF.Identity,
                                 bias=bdt_sb[:, k:k + 1]).then_inc(s_spZ, 1)
                    a.drain()
                    a.activation(t_a[:], t_z[:, zsl], AF.Abs)
                    a.drain()
                    a.activation(t_e[:], t_a[:], AF.Exp, scale=-1.0)
                    a.drain()
                    a.activation(t_l[:, zsl], t_e[:], AF.Ln,
                                 bias=one_c[:]).then_inc(s_spL, 1)
                    g += 1

        @blk.vector
        def _(v):
            v.memset(one_c[:], 1.0)
            g = 0
            for tc in range(NTC):
                cs = slice(tc * 512, (tc + 1) * 512)
                v.wait_ge(s_xp, tc + 1)
                v.tensor_copy(xp_sb[:, cs], p_xp[:, :])
                v.tensor_copy(dtx_b[:, cs], p_xp[0:64, :]).then_inc(s_dtx, 1)
                for k in range(NK):
                    b = g % 2
                    zsl = slice(b * 512, (b + 1) * 512)
                    v.wait_ge(s_spZ, g + 1)
                    v.tensor_scalar(t_r[:], t_z[:, zsl], 0.0, None,
                                    ALU.max).then_inc(s_vr, 1)
                    v.wait_ge(s_spL, g + 1)
                    if g >= 2:
                        v.wait_ge(stp[g % 2], 16 * (g // 2))
                    v.drain()
                    v.tensor_add(dt_out[:, zsl], t_r[:], t_l[:, zsl]).then_inc(s_out, 1)
                    g += 1

        @blk.sync
        def _(sy):
            g = 0
            for tc in range(NTC):
                for k in range(NK):
                    b = g % 2
                    sy.wait_ge(s_out, g + 1)
                    sy.dma_start(out=dtT[k * 128:(k + 1) * 128,
                                         tc * 512:(tc + 1) * 512],
                                 in_=dt_out[:, b * 512:(b + 1) * 512]).then_inc(stp[g % 2], 16)
                    g += 1
            sy.wait_ge(s_dtx, NTC)
            sy.dma_start(out=bc[:], in_=xp_sb[64:96, :]).then_inc(stp[0], 16)
            sy.wait_ge(stp[0], 16 * (NTC * NK // 2 + 1))
            sy.wait_ge(stp[1], 16 * (NTC * NK // 2))
    return nc


def build_E():
    nc = bass.Bass("TRN2", target_bir_lowering=False)
    F16 = mybir.dt.float16
    w2T = nc.dram_tensor("w2T", [FF0, 256], F32, kind="ExternalInput").ap()
    h128 = nc.dram_tensor("h128", [128, 128], BF16, kind="ExternalInput").ap()
    idt = nc.dram_tensor("idt", [128, 128], BF16, kind="ExternalInput").ap()
    gpk = nc.dram_tensor("gpk", [1408, 256], I32, kind="ExternalInput").ap()
    gsc = nc.dram_tensor("gsc", [128, 11 * 16], F16, kind="ExternalInput").ap()
    dpk = nc.dram_tensor("dpk", [256, 704], I32, kind="ExternalInput").ap()
    dsc = nc.dram_tensor("dsc", [128, 2 * 44], F16, kind="ExternalInput").ap()
    w2pT = nc.dram_tensor("w2pT", [FF0, 256], BF16, kind="ExternalOutput").ap()
    wguT = nc.dram_tensor("wguT", [DM, 1408], BF16, kind="ExternalOutput").ap()
    wdnT = nc.dram_tensor("wdnT", [FI, 256], BF16, kind="ExternalOutput").ap()

    SC = float(2.0 ** -6.5)
    NTR = 11 * 16 + 2 * 44

    with ExitStack() as ctx:
        e = ctx.enter_context
        w2s = e(nc.sbuf_tensor("w2s", [128, 2 * 256], BF16))
        t1 = e(nc.sbuf_tensor("t1", [128, 64 * 256], F32))
        tmph = e(nc.sbuf_tensor("tmph", [128, 32 * 256], F32))
        w2o = e(nc.sbuf_tensor("w2o", [128, 64 * 256], BF16))
        h_sb = e(nc.sbuf_tensor("h_sb", [128, 128], BF16))
        idt_sb = e(nc.sbuf_tensor("idt_sb", [128, 128], BF16))
        pk_sb = e(nc.sbuf_tensor("pk_sb", [128, 11 * 256], I32))
        gsc_sb = e(nc.sbuf_tensor("gsc_sb", [128, 11 * 16], F32))
        dpk_sb = e(nc.sbuf_tensor("dpk_sb", [128, 2 * 704], I32))
        dsc_sb = e(nc.sbuf_tensor("dsc_sb", [128, 2 * 44], F32))
        m_i = e(nc.sbuf_tensor("m_i", [128, 704], I32))
        mf = e(nc.sbuf_tensor("mf", [128, 704], F32))
        r1 = e(nc.sbuf_tensor("r1", [128, 704], F32))
        r2 = e(nc.sbuf_tensor("r2", [128, 704], F32))
        uu = e(nc.sbuf_tensor("uu", [128, 704], F32))
        vv = e(nc.sbuf_tensor("vv", [128, 704], F32))
        s31 = e(nc.sbuf_tensor("s31", [128, 704], I32))
        wde = e(nc.sbuf_tensor("wde", [128, 5632], F32))
        wdb = e(nc.sbuf_tensor("wdb", [128, 5632], BF16))
        wtr = e(nc.sbuf_tensor("wtr", [128, 128], BF16))
        p_f = e(nc.psum_tensor("p_f", [128, 256], F32))
        p_tr = e(nc.psum_tensor("p_tr", [128, 128], BF16))
        ld = e(nc.semaphore("ld"))
        lds2 = e(nc.semaphore("lds2"))
        ldwp = [e(nc.semaphore("ldwA")), e(nc.semaphore("ldwB"))]
        s_t = e(nc.semaphore("s_t"))
        s_v = e(nc.semaphore("s_v"))
        s_a = e(nc.semaphore("s_a"))
        st = e(nc.semaphore("st"))
        blk = e(nc.Block())

        @blk.gpsimd
        def _(g):
            g.dma_start(out=gsc_sb[:], in_=gsc[:]).then_inc(ld, 16)
            g.dma_start(out=dsc_sb[:], in_=dsc[:]).then_inc(ld, 16)
            for c in range(64):
                if c >= 2:
                    g.wait_ge(s_t, c - 1)
                b = c % 2
                g.dma_start(out=w2s[:, b * 256:(b + 1) * 256],
                            in_=w2T[c * 128:(c + 1) * 128, :]).then_inc(ldwp[c % 2], 16)

        @blk.sync
        def _(sy):
            sy.dma_start(out=h_sb[:], in_=h128[:]).then_inc(lds2, 16)
            sy.dma_start(out=idt_sb[:], in_=idt[:]).then_inc(lds2, 16)
            sy.dma_start(out=pk_sb[:].rearrange("p (c j) -> p c j", c=11),
                         in_=gpk.rearrange("(c p) j -> p c j", p=128)).then_inc(lds2, 16)
            sy.dma_start(out=dpk_sb[:].rearrange("p (c j) -> p c j", c=2),
                         in_=dpk.rearrange("(c p) j -> p c j", p=128)).then_inc(lds2, 16)

        @blk.tensor
        def _(t):
            t.wait_ge(ld, 16 * 2)
            t.wait_ge(lds2, 16 * 4)
            for c in range(64):
                t.wait_ge(ldwp[c % 2], 16 * (c // 2 + 1))
                if c >= 1:
                    t.wait_ge(s_a, c)
                b = c % 2
                mm = t.matmul(p_f[:], h_sb[:], w2s[:, b * 256:(b + 1) * 256],
                              start=True, stop=True)
                mm.then_inc(s_t, 1)
            ntr = 0
            for c in range(11):
                t.wait_ge(s_v, 2 + c)
                for k in range(16):
                    if ntr >= 1:
                        t.wait_ge(s_a, 64 + ntr)
                    mm = t.transpose(p_tr[:], wdb[:, k * 128:(k + 1) * 128], idt_sb[:])
                    mm.then_inc(s_t, 1)
                    ntr += 1
            for c in range(2):
                t.wait_ge(s_v, 13 + c)
                for k in range(44):
                    t.wait_ge(s_a, 64 + ntr)
                    mm = t.transpose(p_tr[:], wdb[:, k * 128:(k + 1) * 128], idt_sb[:])
                    mm.then_inc(s_t, 1)
                    ntr += 1

        @blk.scalar
        def _(a):
            for c in range(64):
                a.wait_ge(s_t, c + 1)
                a.drain()
                a.activation(t1[:, c * 256:(c + 1) * 256], p_f[:],
                             AF.Identity).then_inc(s_a, 1)
            for ntr in range(NTR):
                a.wait_ge(s_t, 64 + ntr + 1)
                if ntr >= 1:
                    a.wait_ge(st, 16 * (ntr + 1))
                a.drain()
                a.activation(wtr[:], p_tr[:], AF.Identity).then_inc(s_a, 1)

        @blk.vector
        def _(v):
            v.wait_ge(s_a, 64)
            for s in range(6):
                bit = 1 << s
                lo = [c for c in range(64) if not (c & bit)]
                v.drain()
                for j, c in enumerate(lo):
                    v.tensor_copy(tmph[:, j * 256:(j + 1) * 256],
                                  t1[:, c * 256:(c + 1) * 256])
                v.drain()
                for j, c in enumerate(lo):
                    cb = c | bit
                    v.tensor_add(t1[:, c * 256:(c + 1) * 256],
                                 t1[:, c * 256:(c + 1) * 256],
                                 t1[:, cb * 256:(cb + 1) * 256])
                v.drain()
                for j, c in enumerate(lo):
                    cb = c | bit
                    v.tensor_sub(t1[:, cb * 256:(cb + 1) * 256],
                                 tmph[:, j * 256:(j + 1) * 256],
                                 t1[:, cb * 256:(cb + 1) * 256])
            v.drain()
            v.tensor_scalar_mul(w2o[:], t1[:], SC).then_inc(s_v, 1)

            def dequant(pk_ap, n_in, sc_ap, n_grp, base_ms):
                wv = wde[:, 0:n_in].rearrange("p (j e) -> p e j", e=8)
                n8 = n_in // 8
                for p in range(8):
                    v.drain()
                    v.tensor_scalar(m_i[:, 0:n8], pk_ap, 4 * p, 7,
                                    ALU.logical_shift_right, ALU.bitwise_and)
                    v.drain()
                    v.tensor_single_scalar(mf[:, 0:n8], m_i[:, 0:n8], 0.0, ALU.add)
                    v.drain()
                    v.tensor_scalar(r1[:, 0:n8], mf[:, 0:n8], -4.0, 0.0,
                                    ALU.add, ALU.max)
                    v.tensor_scalar(s31[:, 0:n8], pk_ap, 28 - 4 * p,
                                    -2147483648, ALU.logical_shift_left,
                                    ALU.bitwise_and)
                    v.tensor_scalar(r2[:, 0:n8], mf[:, 0:n8], -6.0, 0.0,
                                    ALU.add, ALU.max)
                    v.drain()
                    v.tensor_add(uu[:, 0:n8], mf[:, 0:n8], r1[:, 0:n8])
                    v.drain()
                    v.scalar_tensor_tensor(vv[:, 0:n8], uu[:, 0:n8], 0.5,
                                           r2[:, 0:n8], ALU.mult, ALU.add)
                    v.drain()
                    v.tensor_tensor(wv[:, p, :].bitcast(I32),
                                    vv[:, 0:n8].bitcast(I32),
                                    s31[:, 0:n8], ALU.bitwise_xor)
                v.drain()
                for grp in range(n_grp):
                    v.tensor_scalar_mul(wdb[:, grp * 128:(grp + 1) * 128],
                                        wde[:, grp * 128:(grp + 1) * 128],
                                        sc_ap[:, grp:grp + 1])
                v.drain()
                v.engine_nop().then_inc(s_v, 1)

            for c in range(11):
                if c >= 1:
                    v.wait_ge(s_t, 64 + 16 * c)
                dequant(pk_sb[:, c * 256:(c + 1) * 256], 2048,
                        gsc_sb[:, c * 16:(c + 1) * 16], 16, None)
            for c in range(2):
                v.wait_ge(s_t, 64 + 176 + 44 * c)
                dequant(dpk_sb[:, c * 704:(c + 1) * 704], 5632,
                        dsc_sb[:, c * 44:(c + 1) * 44], 44, None)

        @blk.sync
        def _(sy):
            sy.wait_ge(s_v, 1)
            sy.dma_start(out=w2pT.rearrange("(c p) d -> p c d", p=128),
                         in_=w2o[:].rearrange("p (c d) -> p c d", c=64)).then_inc(st, 16)
            ntr = 0
            for c in range(11):
                for k in range(16):
                    sy.wait_ge(s_a, 64 + ntr + 1)
                    sy.dma_start(out=wguT[k * 128:(k + 1) * 128,
                                          c * 128:(c + 1) * 128],
                                 in_=wtr[:]).then_inc(st, 16)
                    ntr += 1
            for c in range(2):
                for k in range(44):
                    sy.wait_ge(s_a, 64 + ntr + 1)
                    sy.dma_start(out=wdnT[k * 128:(k + 1) * 128,
                                          c * 128:(c + 1) * 128],
                                 in_=wtr[:]).then_inc(st, 16)
                    ntr += 1
            sy.wait_ge(st, 16 * (1 + NTR))
    return nc


def build_C():
    nc = bass.Bass("TRN2", target_bir_lowering=False)
    xa = nc.dram_tensor("xa", [TSL, DM], F32, kind="ExternalInput").ap()
    ya = nc.dram_tensor("ya", [TSL, DM], F32, kind="ExternalInput").ap()
    w1T = nc.dram_tensor("w1T", [DM, FF0], F32, kind="ExternalInput").ap()
    w3T = nc.dram_tensor("w3T", [DM, FF0], F32, kind="ExternalInput").ap()
    w2pT = nc.dram_tensor("w2pT", [FF0, DM], BF16, kind="ExternalInput").ap()
    idt = nc.dram_tensor("idt", [128, 128], BF16, kind="ExternalInput").ap()
    z2o = nc.dram_tensor("z2o", [TSL, DM], F32, kind="ExternalOutput").ap()
    z1f = nc.dram_tensor("z1f", [TSL, DM], F32, kind="Internal").ap()
    hidT = nc.dram_tensor("hidT", [FF0, TSL], BF16, kind="Internal").ap()

    NT = TSL // 128
    NK = DM // 128
    NH0 = FF0 // 128
    INV = 1.0 / DM

    with ExitStack() as ctx:
        e = ctx.enter_context
        zt = e(nc.sbuf_tensor("zt", [128, DM], F32))
        yt_in = e(nc.sbuf_tensor("yt_in", [128, DM], F32))
        z1t = e(nc.sbuf_tensor("z1t", [128, DM], F32))
        z1b = e(nc.sbuf_tensor("z1b", [128, DM], BF16))
        mcol = e(nc.sbuf_tensor("mcol", [128, 4], F32))
        eps_c = e(nc.sbuf_tensor("eps_c", [128, 1], F32))
        idt_sb = e(nc.sbuf_tensor("idt_sb", [128, 128], BF16))
        z1T = e(nc.sbuf_tensor("z1T", [128, NK * TSL], BF16))
        w1_sb = e(nc.sbuf_tensor("w1_sb", [128, 2 * NK * 128], BF16))
        w3_sb = e(nc.sbuf_tensor("w3_sb", [128, 2 * NK * 128], BF16))
        sg = e(nc.sbuf_tensor("sg", [128, 512], BF16))
        hb = e(nc.sbuf_tensor("hb", [128, 2 * 512], BF16))
        hw_sb = e(nc.sbuf_tensor("hw_sb", [128, NH0 * 512], BF16))
        w2_sb = e(nc.sbuf_tensor("w2_sb", [128, 2 * 512], BF16))
        acc3 = [e(nc.sbuf_tensor(f"acc{i}", [128, DM], F32)) for i in range(3)]
        accs = acc3 + [yt_in]
        p_tr = e(nc.psum_tensor("p_tr", [128, 128], BF16))
        p_g = e(nc.psum_tensor("p_g", [128, 512], F32))
        p_u = e(nc.psum_tensor("p_u", [128, 512], F32))
        p_o = [e(nc.psum_tensor(f"p_o{i}", [128, 512], F32)) for i in range(4)]
        ld = e(nc.semaphore("ld"))
        ldi = e(nc.semaphore("ldi"))
        ldwp = [e(nc.semaphore("ldwA")), e(nc.semaphore("ldwB"))]
        ldh = e(nc.semaphore("ldh"))
        ldw2p = [e(nc.semaphore("ldw2A")), e(nc.semaphore("ldw2B"))]
        ldz = e(nc.semaphore("ldz"))
        s_xm = e(nc.semaphore("s_xm"))
        s_sd = e(nc.semaphore("s_sd"))
        s_z1 = e(nc.semaphore("s_z1"))
        s_tr = e(nc.semaphore("s_tr"))
        s_te = e(nc.semaphore("s_te"))
        s_gu = e(nc.semaphore("s_gu"))
        s_sg = e(nc.semaphore("s_sg"))
        s_hb = e(nc.semaphore("s_hb"))
        s_w2c = e(nc.semaphore("s_w2c"))
        s_ev = e(nc.semaphore("s_ev"))
        s_ln = e(nc.semaphore("s_ln"))
        st = e(nc.semaphore("st"))
        sthp = [e(nc.semaphore("sthA")), e(nc.semaphore("sthB"))]
        st2 = e(nc.semaphore("st2"))
        blk = e(nc.Block())

        @blk.sync
        def _(sy):
            sy.dma_start(out=idt_sb[:], in_=idt[:]).then_inc(ldi, 16)
            for tt in range(NT):
                if tt >= 1:
                    sy.wait_ge(s_z1, tt)
                sy.dma_start(out=zt[:],
                             in_=xa[tt * 128:(tt + 1) * 128, :]).then_inc(ld, 16)
                sy.dma_start(out=yt_in[:],
                             in_=ya[tt * 128:(tt + 1) * 128, :]).then_inc(ld, 16)
                sy.wait_ge(s_z1, tt + 1)
                sy.dma_start(out=z1f[tt * 128:(tt + 1) * 128, :],
                             in_=z1t[:]).then_inc(st, 16)

        @blk.vector
        def _(v):
            v.memset(eps_c[:], 1e-5)
            for tt in range(NT):
                v.wait_ge(ld, 16 * (2 * tt + 2))
                v.tensor_add(zt[:], zt[:], yt_in[:])
                v.drain()
                v.tensor_reduce(mcol[:, 0:1], zt[:], AX.X, ALU.add)
                v.drain()
                v.tensor_scalar_mul(mcol[:, 0:1], mcol[:, 0:1], INV)
                v.drain()
                v.tensor_scalar(zt[:], zt[:], mcol[:, 0:1], None,
                                ALU.subtract).then_inc(s_xm, 1)
                v.wait_ge(s_sd, tt + 1)
                v.reciprocal(mcol[:, 2:3], mcol[:, 1:2])
                if tt >= 1:
                    v.wait_ge(st, 16 * tt)
                    v.wait_ge(s_tr, NK * tt)
                v.drain()
                v.tensor_scalar_mul(z1t[:], zt[:], mcol[:, 2:3])
                v.drain()
                v.tensor_copy(z1b[:], z1t[:]).then_inc(s_z1, 1)

        @blk.scalar
        def _(a):
            n = 0
            for tt in range(NT):
                a.wait_ge(s_xm, tt + 1)
                if tt >= 1:
                    a.wait_ge(st, 16 * tt)
                a.drain()
                a.activation(z1t[:], zt[:], AF.Square, accum_out=mcol[:, 3:4])
                a.drain()
                a.activation(mcol[:, 1:2], mcol[:, 3:4], AF.Sqrt, scale=INV,
                             bias=eps_c[:]).then_inc(s_sd, 1)
                for k in range(NK):
                    a.wait_ge(s_tr, n + 1)
                    a.drain()
                    a.activation(z1T[:, k * TSL + tt * 128:k * TSL + (tt + 1) * 128],
                                 p_tr[:], AF.Identity).then_inc(s_te, 1)
                    n += 1

        @blk.tensor
        def _(t):
            t.wait_ge(ldi, 16)
            n = 0
            for tt in range(NT):
                t.wait_ge(s_z1, tt + 1)
                for k in range(NK):
                    if n >= 1:
                        t.wait_ge(s_te, n)
                    mm = t.transpose(p_tr[:], z1b[:, k * 128:(k + 1) * 128],
                                     idt_sb[:])
                    mm.then_inc(s_tr, 1)
                    n += 1
            t.wait_ge(s_te, NT * NK)
            gi = 0
            for hc in range(NH0):
                t.wait_ge(ldwp[hc % 2], 32 * (hc // 2 + 1))
                hb_ = (hc % 2) * NK * 128
                for tcx in range(4):
                    if gi >= 1:
                        t.wait_ge(s_hb, gi)
                    for k in range(NK):
                        t.matmul(p_g[:], w1_sb[:, hb_ + k * 128:hb_ + (k + 1) * 128],
                                 z1T[:, k * TSL + tcx * 512:k * TSL + tcx * 512 + 512],
                                 start=(k == 0), stop=(k == NK - 1))
                    for k in range(NK):
                        mm = t.matmul(p_u[:], w3_sb[:, hb_ + k * 128:hb_ + (k + 1) * 128],
                                      z1T[:, k * TSL + tcx * 512:k * TSL + tcx * 512 + 512],
                                      start=(k == 0), stop=(k == NK - 1))
                    mm.then_inc(s_gu, 1)
                    gi += 1
            for w in range(4):
                t.wait_ge(ldh, 16 * NH0 * (w + 1))
                for dc in range(4):
                    idx = w * 4 + dc
                    if idx > 0:
                        t.wait_ge(s_ev, idx)
                    t.wait_ge(ldz, 16 * (w + 1) * 4)
                    for hc in range(NH0):
                        widx = idx * NH0 + hc
                        t.wait_ge(ldw2p[widx % 2], 16 * (widx // 2 + 1))
                        wb = (widx % 2) * 512
                        for ttl in range(4):
                            mm = t.matmul(p_o[ttl][:],
                                          hw_sb[:, hc * 512 + ttl * 128:hc * 512 + (ttl + 1) * 128],
                                          w2_sb[:, wb:wb + 512],
                                          start=(hc == 0), stop=(hc == NH0 - 1))
                        mm.then_inc(s_w2c, 1)

        @blk.gpsimd
        def _(g):
            for hc in range(NH0):
                if hc >= 2:
                    g.wait_ge(s_gu, 4 * (hc - 1))
                hb_ = (hc % 2) * NK * 128
                g.dma_start(out=w1_sb[:, hb_:hb_ + NK * 128].rearrange(
                                "p (k h) -> p k h", k=NK),
                            in_=w1T.rearrange("(k p) h -> p k h", p=128)[
                                :, :, hc * 128:(hc + 1) * 128]).then_inc(ldwp[hc % 2], 16)
                g.dma_start(out=w3_sb[:, hb_:hb_ + NK * 128].rearrange(
                                "p (k h) -> p k h", k=NK),
                            in_=w3T.rearrange("(k p) h -> p k h", p=128)[
                                :, :, hc * 128:(hc + 1) * 128]).then_inc(ldwp[hc % 2], 16)
            for w in range(4):
                for dc in range(4):
                    idx = w * 4 + dc
                    for hc in range(NH0):
                        widx = idx * NH0 + hc
                        if widx >= 2:
                            g.wait_ge(s_w2c, widx - 1)
                        wb = (widx % 2) * 512
                        g.dma_start(out=w2_sb[:, wb:wb + 512],
                                    in_=w2pT[hc * 128:(hc + 1) * 128,
                                             dc * 512:(dc + 1) * 512]).then_inc(ldw2p[widx % 2], 16)

        @blk.scalar
        def _(a):
            gi = 0
            for hc in range(NH0):
                for tcx in range(4):
                    a.wait_ge(s_gu, gi + 1)
                    if gi >= 1:
                        a.wait_ge(s_hb, gi)
                    a.drain()
                    a.activation(sg[:], p_g[:], AF.Silu).then_inc(s_sg, 1)
                    gi += 1
            for li in range(NT):
                a.wait_ge(s_xm, NT + li + 1)
                if li >= 1:
                    a.wait_ge(st2, 16 * li)
                a.drain()
                a.activation(z1t[:], accs[li % 4][:], AF.Square,
                             accum_out=mcol[:, 3:4])
                a.drain()
                a.activation(mcol[:, 1:2], mcol[:, 3:4], AF.Sqrt, scale=INV,
                             bias=eps_c[:]).then_inc(s_sd, 1)

        @blk.vector
        def _(v):
            gi = 0
            for hc in range(NH0):
                for tcx in range(4):
                    v.wait_ge(s_sg, gi + 1)
                    if gi >= 2:
                        v.wait_ge(sthp[gi % 2], 16 * (gi // 2))
                    v.drain()
                    v.tensor_mul(hb[:, (gi % 2) * 512:(gi % 2 + 1) * 512],
                                 sg[:], p_u[:]).then_inc(s_hb, 1)
                    gi += 1
            for w in range(4):
                for dc in range(4):
                    idx = w * 4 + dc
                    v.wait_ge(s_w2c, (idx + 1) * NH0)
                    v.wait_ge(ldz, 16 * (w + 1) * 4)
                    v.drain()
                    for ttl in range(4):
                        la = v.tensor_add(accs[ttl][:, dc * 512:(dc + 1) * 512],
                                          accs[ttl][:, dc * 512:(dc + 1) * 512],
                                          p_o[ttl][:])
                    la.then_inc(s_ev, 1)
                    v.drain()
                for ttl in range(4):
                    li = w * 4 + ttl
                    acc = accs[ttl]
                    v.tensor_reduce(mcol[:, 0:1], acc[:], AX.X, ALU.add)
                    v.drain()
                    v.tensor_scalar_mul(mcol[:, 0:1], mcol[:, 0:1], INV)
                    v.drain()
                    v.tensor_scalar(acc[:], acc[:], mcol[:, 0:1], None,
                                    ALU.subtract).then_inc(s_xm, 1)
                    v.wait_ge(s_sd, NT + li + 1)
                    v.reciprocal(mcol[:, 2:3], mcol[:, 1:2])
                    if li >= 1:
                        v.wait_ge(st2, 16 * li)
                    v.drain()
                    v.tensor_scalar_mul(z1t[:], acc[:], mcol[:, 2:3]).then_inc(s_ln, 1)

        @blk.sync
        def _(sy):
            gi = 0
            for hc in range(NH0):
                for tcx in range(4):
                    sy.wait_ge(s_hb, gi + 1)
                    sy.dma_start(out=hidT[hc * 128:(hc + 1) * 128,
                                          tcx * 512:(tcx + 1) * 512],
                                 in_=hb[:, (gi % 2) * 512:(gi % 2 + 1) * 512]
                                 ).then_inc(sthp[gi % 2], 16)
                    gi += 1
            sy.wait_ge(sthp[0], 16 * 128)
            sy.wait_ge(sthp[1], 16 * 128)
            sy.wait_ge(st, 16 * NT)
            for w in range(4):
                for hc in range(NH0):
                    if w >= 1:
                        sy.wait_ge(s_w2c, w * 4 * NH0)
                    sy.dma_start(out=hw_sb[:, hc * 512:(hc + 1) * 512],
                                 in_=hidT[hc * 128:(hc + 1) * 128,
                                          w * 512:(w + 1) * 512]).then_inc(ldh, 16)
                for ttl in range(4):
                    if w >= 1:
                        sy.wait_ge(s_ln, w * 4)
                    sy.dma_start(out=accs[ttl][:],
                                 in_=z1f[w * 512 + ttl * 128:w * 512 + (ttl + 1) * 128,
                                         :]).then_inc(ldz, 16)
                for ttl in range(4):
                    li = w * 4 + ttl
                    sy.wait_ge(s_ln, li + 1)
                    sy.dma_start(out=z2o[w * 512 + ttl * 128:w * 512 + (ttl + 1) * 128, :],
                                 in_=z1t[:]).then_inc(st2, 16)
            sy.wait_ge(st2, 16 * NT)
    return nc


def build_B():
    nc = bass.Bass("TRN2", target_bir_lowering=False)
    dtTs = nc.dram_tensor("dtTs", [256, TOK], F32, kind="ExternalInput").ap()
    xTs = nc.dram_tensor("xTs", [256, TOK], F32, kind="ExternalInput").ap()
    Ash = nc.dram_tensor("Ash", [128, 32], F32, kind="ExternalInput").ap()
    Dsh = nc.dram_tensor("Dsh", [128, 2], F32, kind="ExternalInput").ap()
    BT = nc.dram_tensor("BT", [NST, TOK], F32, kind="ExternalInput").ap()
    CT = nc.dram_tensor("CT", [NST, TOK], F32, kind="ExternalInput").ap()
    yTs = nc.dram_tensor("yTs", [256, TOK], F32, kind="ExternalOutput").ap()

    NTC = TOK // 512

    with ExitStack() as ctx:
        e = ctx.enter_context
        btst = e(nc.sbuf_tensor("btst", [1, 2 * NST * 512], F32))
        ctst = e(nc.sbuf_tensor("ctst", [1, 2 * NST * 512], F32))
        alog_sb = e(nc.sbuf_tensor("alog_sb", [128, 32], F32))
        a_sb = e(nc.sbuf_tensor("a_sb", [128, 32], F32))
        d_sb = e(nc.sbuf_tensor("d_sb", [128, 2], F32))
        ones1 = e(nc.sbuf_tensor("ones1", [1, 128], F32))
        dt_t = e(nc.sbuf_tensor("dt_t", [128, 2 * 512], F32))
        x_t = e(nc.sbuf_tensor("x_t", [128, 2 * 512], F32))
        u_sb = e(nc.sbuf_tensor("u_sb", [128, 512], F32))
        dA_sb = e(nc.sbuf_tensor("dA_sb", [128, 2 * 512], F32))
        dbx_sb = e(nc.sbuf_tensor("dbx_sb", [128, 512], F32))
        h_sb = e(nc.sbuf_tensor("h_sb", [128, 512], F32))
        yt_sb = e(nc.sbuf_tensor("yt_sb", [128, 512], F32))
        yacc = e(nc.sbuf_tensor("yacc", [128, 512], F32))
        yo_sb = e(nc.sbuf_tensor("yo_sb", [128, 2 * 512], F32))
        state = e(nc.sbuf_tensor("state", [128, 32], F32))
        p_br = e(nc.psum_tensor("p_br", [128, 2 * 512], F32))
        p_cr = e(nc.psum_tensor("p_cr", [128, 2 * 512], F32))
        ld = e(nc.semaphore("ld"))
        ldtp = [e(nc.semaphore("ldt0")), e(nc.semaphore("ldt1"))]
        ldbp = [e(nc.semaphore("ldb0")), e(nc.semaphore("ldb1"))]
        s_t = e(nc.semaphore("s_t"))
        s_a = e(nc.semaphore("s_a"))
        s_vb = e(nc.semaphore("s_vb"))
        s_vc = e(nc.semaphore("s_vc"))
        s_vs = e(nc.semaphore("s_vs"))
        s_y = e(nc.semaphore("s_y"))
        s_ai = e(nc.semaphore("s_ai"))
        stpB = [e(nc.semaphore("stB0")), e(nc.semaphore("stB1"))]
        blk = e(nc.Block())

        NITER = NTC * 2

        @blk.sync
        def _(sy):
            sy.dma_start(out=alog_sb[:], in_=Ash[:]).then_inc(ld, 16)
            sy.dma_start(out=d_sb[:], in_=Dsh[:]).then_inc(ld, 16)

        @blk.gpsimd
        def _(g):
            for i in range(NITER):
                tc, gr = i // 2, i % 2
                if gr == 0:
                    if tc >= 2:
                        g.wait_ge(s_t, 32 * (tc - 1))
                    sb = slice((tc % 2) * NST * 512, (tc % 2 + 1) * NST * 512)
                    g.dma_start(out=btst[0:1, sb],
                                in_=BT[:, tc * 512:(tc + 1) * 512]).then_inc(ldbp[tc % 2], 16)
                    g.dma_start(out=ctst[0:1, sb],
                                in_=CT[:, tc * 512:(tc + 1) * 512]).then_inc(ldbp[tc % 2], 16)
                if i >= 2:
                    g.wait_ge(s_y, i - 1)
                b = i % 2
                g.dma_start(out=dt_t[:, b * 512:(b + 1) * 512],
                            in_=dtTs[gr * 128:(gr + 1) * 128,
                                     tc * 512:(tc + 1) * 512]).then_inc(ldtp[i % 2], 16)
                g.dma_start(out=x_t[:, b * 512:(b + 1) * 512],
                            in_=xTs[gr * 128:(gr + 1) * 128,
                                    tc * 512:(tc + 1) * 512]).then_inc(ldtp[i % 2], 16)

        @blk.tensor
        def _(t):
            t.wait_ge(s_ai, 1)
            nrep = 0
            for i in range(NITER):
                tc = i // 2
                t.wait_ge(ldbp[tc % 2], 32 * (tc // 2 + 1))
                base = (tc % 2) * NST * 512
                for n in range(NST):
                    nb = nrep % 2
                    psl = slice(nb * 512, (nb + 1) * 512)
                    ssl = slice(base + n * 512, base + (n + 1) * 512)
                    if nrep >= 2:
                        t.wait_ge(s_vb, nrep - 1)
                        t.wait_ge(s_vc, nrep - 1)
                    t.matmul(p_br[:, psl], ones1[:], btst[0:1, ssl],
                             start=True, stop=True)
                    mm = t.matmul(p_cr[:, psl], ones1[:], ctst[0:1, ssl],
                                  start=True, stop=True)
                    mm.then_inc(s_t, 1)
                    nrep += 1

        @blk.scalar
        def _(a):
            a.wait_ge(ld, 16)
            a.activation(a_sb[:], alog_sb[:], AF.Exp).then_inc(s_a, 1)
            a.wait_ge(s_ai, 1)
            nrep = 0
            for i in range(NITER):
                for n in range(NST):
                    nb = nrep % 2
                    a.wait_ge(ldtp[i % 2], 32 * (i // 2 + 1))
                    if nrep >= 2:
                        a.wait_ge(s_vs, nrep - 1)
                    a.drain()
                    a.activation(dA_sb[:, nb * 512:(nb + 1) * 512],
                                 dt_t[:, (i % 2) * 512:(i % 2 + 1) * 512],
                                 AF.Exp,
                                 scale=a_sb[:, (i % 2) * 16 + n:(i % 2) * 16 + n + 1]
                                 ).then_inc(s_a, 1)
                    nrep += 1

        @blk.vector
        def _(v):
            v.memset(ones1[:], 1.0)
            v.memset(state[:], 0.0)
            v.wait_ge(ld, 32)
            v.wait_ge(s_a, 1)
            v.tensor_scalar_mul(a_sb[:], a_sb[:], -1.0).then_inc(s_ai, 1)
            nrep = 0
            for i in range(NITER):
                tc, gr = i // 2, i % 2
                b = i % 2
                bsl = slice(b * 512, (b + 1) * 512)
                v.wait_ge(ldtp[i % 2], 32 * (i // 2 + 1))
                v.tensor_mul(u_sb[:], dt_t[:, bsl], x_t[:, bsl])
                v.drain()
                for n in range(NST):
                    nb = nrep % 2
                    psl = slice(nb * 512, (nb + 1) * 512)
                    scol = gr * 16 + n
                    v.wait_ge(s_t, nrep + 1)
                    v.tensor_mul(dbx_sb[:], u_sb[:], p_br[:, psl]).then_inc(s_vb, 1)
                    v.wait_ge(s_a, nrep + 2)
                    v.drain()
                    init = 0.0 if tc % 16 == 0 else state[:, scol:scol + 1]
                    v.tensor_tensor_scan(h_sb[:], dA_sb[:, psl], dbx_sb[:],
                                         init, ALU.mult, ALU.add).then_inc(s_vs, 1)
                    v.drain()
                    v.tensor_copy(state[:, scol:scol + 1], h_sb[:, 511:512])
                    v.tensor_mul(yt_sb[:], h_sb[:], p_cr[:, psl]).then_inc(s_vc, 1)
                    v.drain()
                    if n == 0:
                        v.tensor_copy(yacc[:], yt_sb[:])
                    else:
                        v.tensor_add(yacc[:], yacc[:], yt_sb[:])
                    v.drain()
                    nrep += 1
                if i >= 2:
                    v.wait_ge(stpB[i % 2], 16 * (i // 2))
                v.scalar_tensor_tensor(yo_sb[:, bsl], x_t[:, bsl],
                                       d_sb[:, gr:gr + 1], yacc[:],
                                       ALU.mult, ALU.add).then_inc(s_y, 1)

        @blk.sync
        def _(sy):
            for i in range(NITER):
                tc, gr = i // 2, i % 2
                b = i % 2
                sy.wait_ge(s_y, i + 1)
                sy.dma_start(out=yTs[gr * 128:(gr + 1) * 128,
                                     tc * 512:(tc + 1) * 512],
                             in_=yo_sb[:, b * 512:(b + 1) * 512]).then_inc(stpB[i % 2], 16)
            sy.wait_ge(stpB[0], 16 * NITER // 2)
            sy.wait_ge(stpB[1], 16 * NITER // 2)
    return nc


def build_D():
    nc = bass.Bass("TRN2", target_bir_lowering=False)
    z2T = nc.dram_tensor("z2T", [DM, SEQ], F32, kind="ExternalInput").ap()
    wq = nc.dram_tensor("wq", [DM, 512], F32, kind="ExternalInput").ap()
    wk = nc.dram_tensor("wk", [DM, 512], F32, kind="ExternalInput").ap()
    wv = nc.dram_tensor("wv", [DM, 512], F32, kind="ExternalInput").ap()
    masks = nc.dram_tensor("masks", [128, 5 * 512], BF16, kind="ExternalInput").ap()
    Oc = nc.dram_tensor("Oc", [512, SEQ], BF16, kind="ExternalOutput").ap()

    NK = DM // 128
    NQC = SEQ // 512
    SCL = float(DH) ** -0.5

    with ExitStack() as ctx:
        e = ctx.enter_context
        z2w = e(nc.sbuf_tensor("z2w", [128, NK * 2048], BF16))
        wq_sb = e(nc.sbuf_tensor("wq_sb", [128, NK * 128], BF16))
        wk_sb = e(nc.sbuf_tensor("wk_sb", [128, NK * 128], BF16))
        wv_sb = e(nc.sbuf_tensor("wv_sb", [128, NK * 128], BF16))
        QT = e(nc.sbuf_tensor("QT", [128, SEQ], BF16))
        KT = e(nc.sbuf_tensor("KT", [128, SEQ], BF16))
        V = e(nc.sbuf_tensor("V", [128, SEQ], BF16))
        msk = e(nc.sbuf_tensor("msk", [128, 5 * 512], BF16))
        pU = e(nc.sbuf_tensor("pU", [128, 2 * 512], BF16))
        pM = e(nc.sbuf_tensor("pM", [128, 4 * 512], BF16))
        rs_sb = e(nc.sbuf_tensor("rs_sb", [1, 512], F32))
        rc_sb = e(nc.sbuf_tensor("rc_sb", [1, 512], F32))
        rep_sb = e(nc.sbuf_tensor("rep_sb", [128, 512], F32))
        Ob = e(nc.sbuf_tensor("Ob", [128, 2 * 512], BF16))
        n20 = e(nc.sbuf_tensor("n20", [128, 1], F32))
        ones_k = e(nc.sbuf_tensor("ones_k", [128, 1], BF16))
        ones_r = e(nc.sbuf_tensor("ones_r", [1, 128], F32))
        p_q = e(nc.psum_tensor("p_q", [128, 2 * 512], F32))
        p_s = e(nc.psum_tensor("p_s", [128, 2 * 512], F32))
        p_rs = e(nc.psum_tensor("p_rs", [1, 512], F32))
        p_o = e(nc.psum_tensor("p_o", [128, 512], F32))
        p_rep = e(nc.psum_tensor("p_rep", [128, 512], F32))
        ld0 = e(nc.semaphore("ld0"))
        ldz = e(nc.semaphore("ldz"))
        ldw = e(nc.semaphore("ldw"))
        s_qkv = e(nc.semaphore("s_qkv"))
        s_qe = e(nc.semaphore("s_qe"))
        s_s = e(nc.semaphore("s_s"))
        s_p = e(nc.semaphore("s_p"))
        s_pm = e(nc.semaphore("s_pm"))
        s_av = e(nc.semaphore("s_av"))
        s_rc = e(nc.semaphore("s_rc"))
        s_rp = e(nc.semaphore("s_rp"))
        s_oe = e(nc.semaphore("s_oe"))
        s_vc = e(nc.semaphore("s_vc"))
        stpD = [e(nc.semaphore("stD0")), e(nc.semaphore("stD1"))]
        blk = e(nc.Block())

        def sched():
            ai = 0
            diag_ai = []
            plan = []
            for h in range(4):
                for qc in range(NQC):
                    for kt in range(4 * qc + 4):
                        diag = kt >= 4 * qc
                        di = len(diag_ai) if diag else -1
                        if diag:
                            diag_ai.append(ai)
                        plan.append((h, qc, kt, diag, ai, di))
                        ai += 1
            return plan, diag_ai
        PLAN, DIAG_AI = sched()
        NAI = len(PLAN)

        @blk.sync
        def _(sy):
            sy.dma_start(out=msk[:], in_=masks[:]).then_inc(ld0, 16)
            oi = 0
            for h in range(4):
                for qc in range(NQC):
                    sy.wait_ge(s_oe, oi + 1)
                    sy.dma_start(out=Oc[h * 128:(h + 1) * 128,
                                        qc * 512:(qc + 1) * 512],
                                 in_=Ob[:, (oi % 2) * 512:(oi % 2 + 1) * 512]
                                 ).then_inc(stpD[oi % 2], 16)
                    oi += 1
            sy.wait_ge(stpD[0], 16 * 32)
            sy.wait_ge(stpD[1], 16 * 32)

        @blk.gpsimd
        def _(g):
            for h in range(4):
                if h >= 1:
                    g.wait_ge(s_oe, 16 * h)
                g.dma_start(out=wq_sb[:].rearrange("p (k j) -> p k j", k=NK),
                            in_=wq.rearrange("(k p) j -> p k j", p=128)[
                                :, :, h * 128:(h + 1) * 128]).then_inc(ldw, 16)
                g.dma_start(out=wk_sb[:].rearrange("p (k j) -> p k j", k=NK),
                            in_=wk.rearrange("(k p) j -> p k j", p=128)[
                                :, :, h * 128:(h + 1) * 128]).then_inc(ldw, 16)
                g.dma_start(out=wv_sb[:].rearrange("p (k j) -> p k j", k=NK),
                            in_=wv.rearrange("(k p) j -> p k j", p=128)[
                                :, :, h * 128:(h + 1) * 128]).then_inc(ldw, 16)
                for w4 in range(4):
                    if 4 * h + w4 >= 1:
                        g.wait_ge(s_qkv, 24 * (4 * h + w4))
                    for k in range(NK):
                        g.dma_start(out=z2w[:, k * 2048:(k + 1) * 2048],
                                    in_=z2T[k * 128:(k + 1) * 128,
                                            w4 * 2048:(w4 + 1) * 2048]
                                    ).then_inc(ldz, 16)

        @blk.tensor
        def _(t):
            t.wait_ge(ld0, 16)
            t.wait_ge(s_vc, 1)
            qi = 0
            for h in range(4):
                t.wait_ge(ldw, 48 * (h + 1))
                for w4 in range(4):
                    t.wait_ge(ldz, 256 * (4 * h + w4 + 1))
                    for which, wsb in ((0, wq_sb), (1, wk_sb)):
                        for tcq in range(4):
                            if qi >= 2:
                                t.wait_ge(s_qe, qi - 1)
                            pb = (qi % 2) * 512
                            for k in range(NK):
                                mm = t.matmul(
                                    p_q[:, pb:pb + 512],
                                    wsb[:, k * 128:(k + 1) * 128],
                                    z2w[:, k * 2048 + tcq * 512:k * 2048 + tcq * 512 + 512],
                                    start=(k == 0), stop=(k == NK - 1))
                            mm.then_inc(s_qkv, 1)
                            qi += 1
                    for tt in range(16):
                        if qi >= 2:
                            t.wait_ge(s_qe, qi - 1)
                        pb = (qi % 2) * 512
                        for k in range(NK):
                            mm = t.matmul(
                                p_q[:, pb:pb + 128],
                                z2w[:, k * 2048 + tt * 128:k * 2048 + (tt + 1) * 128],
                                wv_sb[:, k * 128:(k + 1) * 128],
                                start=(k == 0), stop=(k == NK - 1))
                        mm.then_inc(s_qkv, 1)
                        qi += 1
                for qc in range(NQC):
                    gqc = h * NQC + qc
                    if gqc >= 1:
                        t.wait_ge(s_oe, gqc)
                    nkt = 4 * qc + 4
                    for kt in range(nkt):
                        ai = AI_BASE[(h, qc)] + kt
                        diag = kt >= 4 * qc
                        if ai >= 2:
                            t.wait_ge(s_p, ai - 1)
                        sb_ = (ai % 2) * 512
                        t.matmul(p_s[:, sb_:sb_ + 512],
                                 KT[:, (qc * 0 + kt) * 128:(kt + 1) * 128],
                                 QT[:, qc * 512:(qc + 1) * 512],
                                 start=True, stop=True).then_inc(s_s, 1)
                        if diag:
                            di = DI_OF[ai]
                            t.wait_ge(s_pm, di + 1)
                            rhs = pM[:, (di % 4) * 512:(di % 4 + 1) * 512]
                        else:
                            t.wait_ge(s_p, ai + 1)
                            rhs = pU[:, (ai % 2) * 512:(ai % 2 + 1) * 512]
                        t.matmul(p_rs[:], ones_k[:], rhs,
                                 start=(kt == 0), stop=(kt == nkt - 1))
                        mm = t.matmul(p_o[:], V[:, kt * 128:(kt + 1) * 128], rhs,
                                      start=(kt == 0), stop=(kt == nkt - 1))
                        mm.then_inc(s_av, 1)
                    t.wait_ge(s_rc, gqc + 1)
                    t.matmul(p_rep[:], ones_r[:], rc_sb[:],
                             start=True, stop=True).then_inc(s_rp, 1)

        @blk.scalar
        def _(a):
            qi = 0
            for h in range(4):
                for w4 in range(4):
                    for which in range(2):
                        for tcq in range(4):
                            a.wait_ge(s_qkv, qi + 1)
                            a.drain()
                            dst = (QT if which == 0 else KT)[
                                :, w4 * 2048 + tcq * 512:w4 * 2048 + tcq * 512 + 512]
                            pb = (qi % 2) * 512
                            if which == 0:
                                a.activation(dst, p_q[:, pb:pb + 512],
                                             AF.Identity, scale=SCL).then_inc(s_qe, 1)
                            else:
                                a.activation(dst, p_q[:, pb:pb + 512],
                                             AF.Identity).then_inc(s_qe, 1)
                            qi += 1
                    for tt in range(16):
                        a.wait_ge(s_qkv, qi + 1)
                        a.drain()
                        pb = (qi % 2) * 512
                        a.activation(V[:, (w4 * 16 + tt) * 128:(w4 * 16 + tt + 1) * 128],
                                     p_q[:, pb:pb + 128],
                                     AF.Identity).then_inc(s_qe, 1)
                        qi += 1
                for qc in range(NQC):
                    nkt = 4 * qc + 4
                    for kt in range(nkt):
                        ai = AI_BASE[(h, qc)] + kt
                        a.wait_ge(s_s, ai + 1)
                        if ai >= 2:
                            a.wait_ge(s_av, ai - 1)
                        sb_ = (ai % 2) * 512
                        a.drain()
                        a.activation(pU[:, (ai % 2) * 512:(ai % 2 + 1) * 512],
                                     p_s[:, sb_:sb_ + 512], AF.Exp,
                                     bias=n20[:]).then_inc(s_p, 1)

        @blk.vector
        def _(v):
            v.memset(n20[:], -20.0)
            v.memset(ones_k[:], 1.0)
            v.memset(ones_r[:], 1.0).then_inc(s_vc, 1)
            v.wait_ge(ld0, 16)
            for h in range(4):
                for qc in range(NQC):
                    gqc = h * NQC + qc
                    nkt = 4 * qc + 4
                    for kt in range(nkt):
                        ai = AI_BASE[(h, qc)] + kt
                        diag = kt >= 4 * qc
                        if not diag:
                            continue
                        di = DI_OF[ai]
                        r = kt - 4 * qc
                        mc = 4 if (qc == 0 and kt == 0) else r
                        v.wait_ge(s_p, ai + 1)
                        if di >= 4:
                            v.wait_ge(s_av, DIAG_AI[di - 4] + 1)
                        v.drain()
                        v.tensor_mul(pM[:, (di % 4) * 512:(di % 4 + 1) * 512],
                                     pU[:, (ai % 2) * 512:(ai % 2 + 1) * 512],
                                     msk[:, mc * 512:(mc + 1) * 512]).then_inc(s_pm, 1)
                    ai_last = AI_BASE[(h, qc)] + nkt - 1
                    v.wait_ge(s_av, ai_last + 1)
                    v.drain()
                    v.tensor_copy(rs_sb[:], p_rs[:])
                    v.drain()
                    v.reciprocal(rc_sb[:], rs_sb[:]).then_inc(s_rc, 1)
                    v.wait_ge(s_rp, gqc + 1)
                    v.drain()
                    v.tensor_copy(rep_sb[:], p_rep[:])
                    if gqc >= 2:
                        v.wait_ge(stpD[gqc % 2], 16 * (gqc // 2))
                    v.drain()
                    v.tensor_mul(Ob[:, (gqc % 2) * 512:(gqc % 2 + 1) * 512],
                                 p_o[:], rep_sb[:]).then_inc(s_oe, 1)
    return nc


AI_BASE = {}
DI_OF = {}
def _init_attn_sched():
    ai = 0
    di = 0
    for h in range(4):
        for qc in range(16):
            AI_BASE[(h, qc)] = ai
            for kt in range(4 * qc + 4):
                if kt >= 4 * qc:
                    DI_OF[ai] = di
                    di += 1
                ai += 1
_init_attn_sched()


def build_F():
    nc = bass.Bass("TRN2", target_bir_lowering=False)
    OTs = nc.dram_tensor("OTs", [DM, TSL], BF16, kind="ExternalInput").ap()
    z2s = nc.dram_tensor("z2s", [TSL, DM], F32, kind="ExternalInput").ap()
    wout = nc.dram_tensor("wout", [DM, DM], F32, kind="ExternalInput").ap()
    boutr = nc.dram_tensor("boutr", [128, DM], F32, kind="ExternalInput").ap()
    wguT = nc.dram_tensor("wguT", [DM, 2 * FI], BF16, kind="ExternalInput").ap()
    wdnT = nc.dram_tensor("wdnT", [FI, DM], BF16, kind="ExternalInput").ap()
    idt = nc.dram_tensor("idt", [128, 128], BF16, kind="ExternalInput").ap()
    outp = nc.dram_tensor("outp", [TSL, DM], F32, kind="ExternalOutput").ap()
    x3f = nc.dram_tensor("x3f", [TSL, DM], F32, kind="Internal").ap()
    x3Tf = nc.dram_tensor("x3Tf", [DM, TSL], BF16, kind="Internal").ap()

    NT = TSL // 128
    NK = DM // 128
    NPC = FI // 128
    INV = 1.0 / DM
    NWN = 4

    with ExitStack() as ctx:
        e = ctx.enter_context
        big = e(nc.sbuf_tensor("big", [128, NK * DM], BF16))
        x3w = e(nc.sbuf_tensor("x3w", [128, NK * 512], BF16))
        x3ev = e(nc.sbuf_tensor("x3ev", [128, NK * 128], BF16))
        otile = e(nc.sbuf_tensor("otile", [128, NK * 128], BF16))
        z2t = e(nc.sbuf_tensor("z2t", [128, DM], F32))
        xrd = e(nc.sbuf_tensor("xrd", [128, DM], F32))
        x3b = e(nc.sbuf_tensor("x3b", [128, DM], BF16))
        accs = [e(nc.sbuf_tensor(f"acc{i}", [128, DM], F32)) for i in range(4)]
        outt = e(nc.sbuf_tensor("outt", [128, 2 * DM], F32))
        wg_sb = e(nc.sbuf_tensor("wg_sb", [128, 2 * NK * 128], BF16))
        wu_sb = e(nc.sbuf_tensor("wu_sb", [128, 2 * NK * 128], BF16))
        wd_sb = e(nc.sbuf_tensor("wd_sb", [128, 2 * 512], BF16))
        sg = e(nc.sbuf_tensor("sg", [128, 512], BF16))
        mcol = e(nc.sbuf_tensor("mcol", [128, 4], F32))
        eps5 = e(nc.sbuf_tensor("eps5", [128, 1], F32))
        eps6 = e(nc.sbuf_tensor("eps6", [128, 1], F32))
        idt_sb = e(nc.sbuf_tensor("idt_sb", [128, 128], BF16))
        p_op = e(nc.psum_tensor("p_op", [128, 512], F32))
        p_tr = e(nc.psum_tensor("p_tr", [128, 128], BF16))
        p_g = e(nc.psum_tensor("p_g", [128, 512], F32))
        p_u = e(nc.psum_tensor("p_u", [128, 512], F32))
        p_d = [e(nc.psum_tensor(f"p_d{i}", [128, 512], F32)) for i in range(4)]
        ld0 = e(nc.semaphore("ld0"))
        ldwt = e(nc.semaphore("ldwt"))
        ld1 = e(nc.semaphore("ld1"))
        ldgp = [e(nc.semaphore("ldgA")), e(nc.semaphore("ldgB"))]
        lddp = [e(nc.semaphore("lddA")), e(nc.semaphore("lddB"))]
        ldx = e(nc.semaphore("ldx"))
        s_op = e(nc.semaphore("s_op"))
        s_ev1 = e(nc.semaphore("s_ev1"))
        s_e1 = e(nc.semaphore("s_e1"))
        s_xm = e(nc.semaphore("s_xm"))
        s_sd = e(nc.semaphore("s_sd"))
        s_tr = e(nc.semaphore("s_tr"))
        s_te = e(nc.semaphore("s_te"))
        s_gu = e(nc.semaphore("s_gu"))
        s_sg = e(nc.semaphore("s_sg"))
        s_hb = e(nc.semaphore("s_hb"))
        s_dc = e(nc.semaphore("s_dc"))
        s_ev2 = e(nc.semaphore("s_ev2"))
        s_rm = e(nc.semaphore("s_rm"))
        s_fin = e(nc.semaphore("s_fin"))
        st1 = e(nc.semaphore("st1"))
        stx = e(nc.semaphore("stx"))
        ldxw = e(nc.semaphore("ldxw"))
        st2p = [e(nc.semaphore("st2A")), e(nc.semaphore("st2B"))]
        blk = e(nc.Block())

        @blk.sync
        def _(sy):
            sy.dma_start(out=idt_sb[:], in_=idt[:]).then_inc(ld0, 16)
            sy.dma_start(out=xrd[:], in_=boutr[:]).then_inc(ld0, 16)
            for tt in range(NT):
                if tt >= 1:
                    sy.wait_ge(s_e1, tt)
                sy.dma_start(out=otile[:].rearrange("p (k t) -> p k t", k=NK),
                             in_=OTs.rearrange("(k p) t -> p k t", p=128)[
                                 :, :, tt * 128:(tt + 1) * 128]).then_inc(ld1, 16)
                sy.dma_start(out=z2t[:],
                             in_=z2s[tt * 128:(tt + 1) * 128, :]).then_inc(ld1, 16)
                sy.wait_ge(s_e1, tt + 1)
                sy.dma_start(out=x3f[tt * 128:(tt + 1) * 128, :],
                             in_=accs[0][:]).then_inc(st1, 16)
                sy.wait_ge(s_te, NK * (tt + 1))
                sy.dma_start(out=x3Tf.rearrange("(k p) t -> p k t", p=128)[
                                 :, :, tt * 128:(tt + 1) * 128],
                             in_=x3ev[:].rearrange("p (k t) -> p k t", k=NK)
                             ).then_inc(stx, 16)
            sy.wait_ge(st1, 16 * NT)
            sy.wait_ge(stx, 16 * NT)
            li = 0
            for wn in range(NWN):
                if wn >= 1:
                    sy.wait_ge(s_gu, wn * NPC)
                sy.dma_start(out=x3w[:].rearrange("p (k t) -> p k t", k=NK),
                             in_=x3Tf.rearrange("(k p) t -> p k t", p=128)[
                                 :, :, wn * 512:(wn + 1) * 512]).then_inc(ldxw, 16)
                for ttl in range(4):
                    if li >= 1:
                        sy.wait_ge(s_xm, NT + li)
                    sy.dma_start(out=xrd[:],
                                 in_=x3f[wn * 512 + ttl * 128:wn * 512 + (ttl + 1) * 128, :]
                                 ).then_inc(ldx, 16)
                    sy.wait_ge(s_fin, li + 1)
                    sy.dma_start(out=outp[wn * 512 + ttl * 128:wn * 512 + (ttl + 1) * 128, :],
                                 in_=outt[:, (li % 2) * DM:(li % 2 + 1) * DM]
                                 ).then_inc(st2p[li % 2], 16)
                    li += 1
            sy.wait_ge(st2p[0], 16 * NT // 2)
            sy.wait_ge(st2p[1], 16 * NT // 2)

        @blk.gpsimd
        def _(g):
            g.dma_start(out=big[:].rearrange("p (k d) -> p k d", k=NK),
                        in_=wout.rearrange("(k p) d -> p k d", p=128)).then_inc(ldwt, 16)
            for wn in range(NWN):
                for pc in range(NPC):
                    gi = wn * NPC + pc
                    if gi >= 2:
                        g.wait_ge(s_gu, gi - 1)
                    bb = (gi % 2) * NK * 128
                    g.dma_start(out=wg_sb[:, bb:bb + NK * 128].rearrange(
                                    "p (k j) -> p k j", k=NK),
                                in_=wguT.rearrange("(k p) o -> p k o", p=128)[
                                    :, :, pc * 128:(pc + 1) * 128]).then_inc(ldgp[gi % 2], 16)
                    g.dma_start(out=wu_sb[:, bb:bb + NK * 128].rearrange(
                                    "p (k j) -> p k j", k=NK),
                                in_=wguT.rearrange("(k p) o -> p k o", p=128)[
                                    :, :, FI + pc * 128:FI + (pc + 1) * 128]).then_inc(ldgp[gi % 2], 16)
                for dc in range(4):
                    for ic in range(NPC):
                        widx = (wn * 4 + dc) * NPC + ic
                        if widx >= 2:
                            g.wait_ge(s_dc, widx - 1)
                        wb = (widx % 2) * 512
                        g.dma_start(out=wd_sb[:, wb:wb + 512],
                                    in_=wdnT[ic * 128:(ic + 1) * 128,
                                             dc * 512:(dc + 1) * 512]).then_inc(lddp[widx % 2], 16)

        @blk.tensor
        def _(t):
            t.wait_ge(ld0, 32)
            t.wait_ge(ldwt, 16)
            n = 0
            for tt in range(NT):
                t.wait_ge(ld1, 32 * (tt + 1))
                for dc in range(4):
                    if tt + dc >= 1:
                        t.wait_ge(s_ev1, 4 * tt + dc)
                    for k in range(NK):
                        mm = t.matmul(p_op[:],
                                      otile[:, k * 128:(k + 1) * 128],
                                      big[:, k * DM + dc * 512:k * DM + dc * 512 + 512],
                                      start=(k == 0), stop=(k == NK - 1))
                    mm.then_inc(s_op, 1)
                t.wait_ge(s_e1, tt + 1)
                for k in range(NK):
                    if n >= 1:
                        t.wait_ge(s_te, n)
                    mm = t.transpose(p_tr[:], x3b[:, k * 128:(k + 1) * 128],
                                     idt_sb[:])
                    mm.then_inc(s_tr, 1)
                    n += 1
            for wn in range(NWN):
                t.wait_ge(ldxw, 16 * (wn + 1))
                for pc in range(NPC):
                    gi = wn * NPC + pc
                    t.wait_ge(ldgp[gi % 2], 32 * (gi // 2 + 1))
                    if gi >= 1:
                        t.wait_ge(s_hb, gi)
                    bb = (gi % 2) * NK * 128
                    for k in range(NK):
                        t.matmul(p_g[:], wg_sb[:, bb + k * 128:bb + (k + 1) * 128],
                                 x3w[:, k * 512:(k + 1) * 512],
                                 start=(k == 0), stop=(k == NK - 1))
                    for k in range(NK):
                        mm = t.matmul(p_u[:], wu_sb[:, bb + k * 128:bb + (k + 1) * 128],
                                      x3w[:, k * 512:(k + 1) * 512],
                                      start=(k == 0), stop=(k == NK - 1))
                    mm.then_inc(s_gu, 1)
                for dc in range(4):
                    wdi = wn * 4 + dc
                    if wdi >= 1:
                        t.wait_ge(s_ev2, wdi)
                    for ic in range(NPC):
                        widx = wdi * NPC + ic
                        t.wait_ge(lddp[widx % 2], 16 * (widx // 2 + 1))
                        t.wait_ge(s_hb, wn * NPC + ic + 1)
                        wb = (widx % 2) * 512
                        for ttl in range(4):
                            mm = t.matmul(p_d[ttl][:],
                                          big[:, ic * 512 + ttl * 128:ic * 512 + (ttl + 1) * 128],
                                          wd_sb[:, wb:wb + 512],
                                          start=(ic == 0), stop=(ic == NPC - 1))
                        mm.then_inc(s_dc, 1)

        @blk.scalar
        def _(a):
            n = 0
            for tt in range(NT):
                a.wait_ge(s_xm, tt + 1)
                if tt >= 1:
                    a.wait_ge(st1, 16 * tt)
                a.drain()
                a.activation(z2t[:], accs[0][:], AF.Square, accum_out=mcol[:, 3:4])
                a.drain()
                a.activation(mcol[:, 1:2], mcol[:, 3:4], AF.Sqrt, scale=INV,
                             bias=eps5[:]).then_inc(s_sd, 1)
                if tt >= 1:
                    a.wait_ge(stx, 16 * tt)
                for k in range(NK):
                    a.wait_ge(s_tr, n + 1)
                    a.drain()
                    a.activation(x3ev[:, k * 128:(k + 1) * 128],
                                 p_tr[:], AF.Identity).then_inc(s_te, 1)
                    n += 1
            for wn in range(NWN):
                for pc in range(NPC):
                    gi = wn * NPC + pc
                    a.wait_ge(s_gu, gi + 1)
                    if gi >= 1:
                        a.wait_ge(s_hb, gi)
                    a.drain()
                    a.activation(sg[:], p_g[:], AF.Silu).then_inc(s_sg, 1)
                for ttl in range(4):
                    li = wn * 4 + ttl
                    a.wait_ge(s_ev2, 4 * (wn + 1))
                    a.wait_ge(s_fin, li)
                    a.drain()
                    a.activation(z2t[:], accs[ttl][:], AF.Square,
                                 accum_out=mcol[:, 3:4])
                    a.drain()
                    a.activation(mcol[:, 1:2], mcol[:, 3:4], AF.Sqrt, scale=1.0 / DM,
                                 bias=eps6[:]).then_inc(s_rm, 1)
                    a.wait_ge(s_xm, NT + li + 1)
                    a.drain()
                    a.activation(z2t[:], accs[ttl][:], AF.Square,
                                 accum_out=mcol[:, 3:4])
                    a.drain()
                    a.activation(mcol[:, 1:2], mcol[:, 3:4], AF.Sqrt, scale=INV,
                                 bias=eps5[:]).then_inc(s_sd, 1)

        @blk.vector
        def _(v):
            v.memset(eps5[:], 1e-5)
            v.memset(eps6[:], 1e-6)
            for tt in range(NT):
                acc = accs[0]
                if tt >= 1:
                    v.wait_ge(st1, 16 * tt)
                for dc in range(4):
                    v.wait_ge(s_op, 4 * tt + dc + 1)
                    v.drain()
                    v.tensor_add(acc[:, dc * 512:(dc + 1) * 512], p_op[:],
                                 xrd[:, dc * 512:(dc + 1) * 512]).then_inc(s_ev1, 1)
                v.wait_ge(ld1, 32 * (tt + 1))
                v.drain()
                v.tensor_add(acc[:], acc[:], z2t[:])
                v.drain()
                v.tensor_reduce(mcol[:, 0:1], acc[:], AX.X, ALU.add)
                v.drain()
                v.tensor_scalar_mul(mcol[:, 0:1], mcol[:, 0:1], INV)
                v.drain()
                v.tensor_scalar(acc[:], acc[:], mcol[:, 0:1], None,
                                ALU.subtract).then_inc(s_xm, 1)
                v.wait_ge(s_sd, tt + 1)
                v.reciprocal(mcol[:, 2:3], mcol[:, 1:2])
                if tt >= 1:
                    v.wait_ge(s_tr, NK * tt)
                v.drain()
                v.tensor_scalar_mul(acc[:], acc[:], mcol[:, 2:3])
                v.drain()
                v.tensor_copy(x3b[:], acc[:]).then_inc(s_e1, 1)
            for wn in range(NWN):
                for pc in range(NPC):
                    gi = wn * NPC + pc
                    v.wait_ge(s_sg, gi + 1)
                    if wn >= 1:
                        pass
                    if wn >= 1 and pc == 0:
                        v.wait_ge(s_dc, wn * 4 * NPC)
                    v.drain()
                    v.tensor_mul(big[:, pc * 512:(pc + 1) * 512], sg[:], p_u[:]
                                 ).then_inc(s_hb, 1)
                for dc in range(4):
                    wdi = wn * 4 + dc
                    v.wait_ge(s_dc, (wdi + 1) * NPC)
                    v.drain()
                    for ttl in range(4):
                        la = v.tensor_copy(accs[ttl][:, dc * 512:(dc + 1) * 512],
                                           p_d[ttl][:])
                    la.then_inc(s_ev2, 1)
                    v.drain()
                for ttl in range(4):
                    li = wn * 4 + ttl
                    acc = accs[ttl]
                    v.wait_ge(s_rm, li + 1)
                    v.reciprocal(mcol[:, 2:3], mcol[:, 1:2])
                    v.drain()
                    v.tensor_scalar_mul(acc[:], acc[:], mcol[:, 2:3])
                    v.wait_ge(ldx, 16 * (li + 1))
                    v.drain()
                    v.tensor_add(acc[:], acc[:], xrd[:])
                    v.drain()
                    v.tensor_reduce(mcol[:, 0:1], acc[:], AX.X, ALU.add)
                    v.drain()
                    v.tensor_scalar_mul(mcol[:, 0:1], mcol[:, 0:1], INV)
                    v.drain()
                    v.tensor_scalar(acc[:], acc[:], mcol[:, 0:1], None,
                                    ALU.subtract).then_inc(s_xm, 1)
                    v.wait_ge(s_sd, NT + li + 1)
                    v.reciprocal(mcol[:, 2:3], mcol[:, 1:2])
                    if li >= 2:
                        v.wait_ge(st2p[li % 2], 16 * (li // 2))
                    v.drain()
                    v.tensor_scalar_mul(outt[:, (li % 2) * DM:(li % 2 + 1) * DM],
                                        acc[:], mcol[:, 2:3]).then_inc(s_fin, 1)
    return nc


_BUILT = {}


def _get(name, fn):
    if name not in _BUILT:
        _BUILT[name] = fn()
    return _BUILT[name]


def _hadamard128():
    H = np.array([[1.0]], dtype=np.float32)
    while H.shape[0] < 128:
        H = np.block([[H, H], [H, -H]])
    return H


def kernel(**ins):
    x = np.ascontiguousarray(ins["L0_attn_x"].reshape(TOK, DM))
    xT = np.ascontiguousarray(x.T)
    wxT = np.ascontiguousarray(ins["L0_attn_W_x"].T)
    wdtT = np.ascontiguousarray(ins["L0_attn_W_dt"].T)
    bdt = np.ascontiguousarray(ins["L0_attn_b_dt"].reshape(16, 128).T)
    idt = np.eye(128, dtype=BF)
    h128 = np.ascontiguousarray(_hadamard128().astype(BF))
    cores = list(range(NCORE))

    W2T = np.ascontiguousarray(ins["L0_ffn_W2"].T)
    gup = ins["L1_ffn_gate_up_packed"]
    gscT = np.ascontiguousarray(ins["L1_ffn_gate_up_scales"].T)
    dpk = ins["L1_ffn_down_packed"]
    dscT = np.ascontiguousarray(ins["L1_ffn_down_scales"].T)
    ncE = _get("E", build_E)
    in_maps = []
    for c in range(NCORE):
        dsl = slice(c * 256, (c + 1) * 256)
        gpk_c = np.ascontiguousarray(np.concatenate(
            [gup[c * 704:(c + 1) * 704],
             gup[FI + c * 704:FI + (c + 1) * 704]], axis=0))
        gsc_c = np.ascontiguousarray(np.concatenate(
            [gscT[c * 704:(c + 1) * 704],
             gscT[FI + c * 704:FI + (c + 1) * 704]], axis=0)
            .reshape(11, 128, 16).transpose(1, 0, 2).reshape(128, 176))
        dpk_c = np.ascontiguousarray(dpk[c * 256:(c + 1) * 256])
        dsc_c = np.ascontiguousarray(dscT[c * 256:(c + 1) * 256]
                                     .reshape(2, 128, 44).transpose(1, 0, 2)
                                     .reshape(128, 88))
        in_maps.append(dict(w2T=np.ascontiguousarray(W2T[:, dsl]), h128=h128,
                            idt=idt, gpk=gpk_c, gsc=gsc_c, dpk=dpk_c, dsc=dsc_c))
    resE = run_bass_kernel_spmd(ncE, in_maps, core_ids=cores)
    w2pT = np.concatenate([r["w2pT"] for r in resE.results], axis=1)
    wguT = np.zeros((DM, 2 * FI), dtype=BF)
    wdnT = np.concatenate([r["wdnT"] for r in resE.results], axis=1)
    for c in range(NCORE):
        blkc = resE.results[c]["wguT"]
        wguT[:, c * 704:(c + 1) * 704] = blkc[:, 0:704]
        wguT[:, FI + c * 704:FI + (c + 1) * 704] = blkc[:, 704:1408]

    ncA = _get("A", build_A)
    in_maps = []
    for c in range(NCORE):
        in_maps.append(dict(
            xT=np.ascontiguousarray(xT[:, c * TSL:(c + 1) * TSL]),
            wxT=wxT, wdtT=wdtT, bdt=bdt))
    resA = run_bass_kernel_spmd(ncA, in_maps, core_ids=cores)
    dtT = np.concatenate([r["dtT"] for r in resA.results], axis=1)
    bcs = np.concatenate([r["bc"] for r in resA.results], axis=1)
    BT = np.ascontiguousarray(bcs[0:16])
    CT = np.ascontiguousarray(bcs[16:32])

    A_log = ins["L0_attn_A_log"]
    Dv = ins["L0_attn_D"]
    ncB = _get("B", build_B)
    in_maps = []
    for c in range(NCORE):
        rs = slice(c * 256, (c + 1) * 256)
        in_maps.append(dict(
            dtTs=np.ascontiguousarray(dtT[rs]),
            xTs=np.ascontiguousarray(xT[rs]),
            Ash=np.ascontiguousarray(
                A_log[rs].reshape(2, 128, NST).transpose(1, 0, 2).reshape(128, 32)),
            Dsh=np.ascontiguousarray(Dv[rs].reshape(2, 128).T),
            BT=BT, CT=CT))
    resB = run_bass_kernel_spmd(ncB, in_maps, core_ids=cores)
    yT = np.concatenate([r["yTs"] for r in resB.results], axis=0)
    y = np.ascontiguousarray(yT.T)

    w1T_h = np.ascontiguousarray(ins["L0_ffn_W1"].T)
    w3T_h = np.ascontiguousarray(ins["L0_ffn_W3"].T)
    ncC = _get("C", build_C)
    in_maps = []
    for c in range(NCORE):
        ts = slice(c * TSL, (c + 1) * TSL)
        in_maps.append(dict(xa=np.ascontiguousarray(x[ts]),
                            ya=np.ascontiguousarray(y[ts]),
                            w1T=w1T_h, w3T=w3T_h, w2pT=w2pT, idt=idt))
    resC = run_bass_kernel_spmd(ncC, in_maps, core_ids=cores)
    z2 = np.concatenate([r["z2o"] for r in resC.results], axis=0)

    wqkv = ins["L1_attn_w_qkv"]
    masks_h = _attn_masks()
    ncD = _get("D", build_D)
    in_maps = []
    for c in range(NCORE):
        beta = c // 4
        h0 = (c % 4) * 4
        z2T_b = np.ascontiguousarray(z2[beta * SEQ:(beta + 1) * SEQ].T)
        in_maps.append(dict(
            z2T=z2T_b,
            wq=np.ascontiguousarray(wqkv[:, h0 * DH:(h0 + 4) * DH]),
            wk=np.ascontiguousarray(wqkv[:, DM + h0 * DH:DM + (h0 + 4) * DH]),
            wv=np.ascontiguousarray(wqkv[:, 2 * DM + h0 * DH:2 * DM + (h0 + 4) * DH]),
            masks=masks_h))
    resD = run_bass_kernel_spmd(ncD, in_maps, core_ids=cores)
    OT = [np.concatenate([resD.results[b * 4 + j]["Oc"] for j in range(4)], axis=0)
          for b in range(2)]
    OTfull = np.concatenate(OT, axis=1)

    wout_h = np.ascontiguousarray(ins["L1_attn_w_out"])
    boutr_h = np.ascontiguousarray(np.tile(ins["L1_attn_b_out"][None, :], (128, 1)))
    ncF = _get("F", build_F)
    in_maps = []
    for c in range(NCORE):
        ts = slice(c * TSL, (c + 1) * TSL)
        in_maps.append(dict(
            OTs=np.ascontiguousarray(OTfull[:, ts]),
            z2s=np.ascontiguousarray(z2[ts]),
            wout=wout_h, boutr=boutr_h, wguT=wguT, wdnT=wdnT, idt=idt))
    resF = run_bass_kernel_spmd(ncF, in_maps, core_ids=cores)
    out = np.concatenate([r["outp"] for r in resF.results], axis=0)
    return out.reshape(2, SEQ, DM)


def _attn_masks():
    m = np.zeros((128, 5 * 512), dtype=BF)
    kk = np.arange(128)[:, None]
    qq = np.arange(512)[None, :]
    for r in range(4):
        m[:, r * 512:(r + 1) * 512] = (kk + r * 128 <= qq).astype(BF)
    corner = ((kk <= qq) | (qq < 128)).astype(BF)
    m[:, 4 * 512:5 * 512] = corner
    return m


# version 02
# speedup vs baseline: 1.4087x; 1.4087x over previous
import numpy as np
import ml_dtypes
from contextlib import ExitStack

import concourse.bass as bass
import concourse.mybir as mybir
from concourse.bass_utils import run_bass_kernel_spmd

F32 = mybir.dt.float32
BF16 = mybir.dt.bfloat16
I32 = mybir.dt.int32
AF = mybir.ActivationFunctionType
ALU = mybir.AluOpType
AX = mybir.AxisListType
BF = ml_dtypes.bfloat16

NCORE = 8
DM = 2048
TOK = 16384
TSL = TOK // NCORE
NST = 16
DTR = 64
SEQ = 8192
NH = 16
DH = 128
FF0 = 8192
FI = 5632


def build_A():
    nc = bass.Bass("TRN2", target_bir_lowering=False)
    xT = nc.dram_tensor("xT", [DM, TSL], F32, kind="ExternalInput").ap()
    wxT = nc.dram_tensor("wxT", [DM, 96], F32, kind="ExternalInput").ap()
    wdtT = nc.dram_tensor("wdtT", [DTR, DM], F32, kind="ExternalInput").ap()
    bdt = nc.dram_tensor("bdt", [128, 16], F32, kind="ExternalInput").ap()
    dtT = nc.dram_tensor("dtT", [DM, TSL], F32, kind="ExternalOutput").ap()
    bc = nc.dram_tensor("bc", [32, TSL], F32, kind="ExternalOutput").ap()

    NK = DM // 128
    NTC = TSL // 512

    with ExitStack() as ctx:
        e = ctx.enter_context
        xb = e(nc.sbuf_tensor("xb", [128, NK * TSL], BF16))
        wx_sb = e(nc.sbuf_tensor("wx_sb", [128, NK * 96], BF16))
        wdt_sb = e(nc.sbuf_tensor("wdt_sb", [DTR, DM], BF16))
        bdt_sb = e(nc.sbuf_tensor("bdt_sb", [128, 16], F32))
        xp_sb = e(nc.sbuf_tensor("xp_sb", [96, TSL], F32))
        dtx_b = e(nc.sbuf_tensor("dtx_b", [DTR, TSL], BF16))
        t_z = e(nc.sbuf_tensor("t_z", [128, 2 * 512], F32))
        t_a = e(nc.sbuf_tensor("t_a", [128, 512], F32))
        t_e = e(nc.sbuf_tensor("t_e", [128, 512], F32))
        t_l = e(nc.sbuf_tensor("t_l", [128, 2 * 512], F32))
        t_r = e(nc.sbuf_tensor("t_r", [128, 512], F32))
        dt_out = e(nc.sbuf_tensor("dt_out", [128, 2 * 512], F32))
        one_c = e(nc.sbuf_tensor("one_c", [128, 1], F32))
        p_xp = e(nc.psum_tensor("p_xp", [96, 512], F32))
        p_dt = e(nc.psum_tensor("p_dt", [128, 2 * 512], F32))
        ld = e(nc.semaphore("ld"))
        lds = e(nc.semaphore("lds"))
        s_xp = e(nc.semaphore("s_xp"))
        s_dt = e(nc.semaphore("s_dt"))
        s_dtx = e(nc.semaphore("s_dtx"))
        s_spZ = e(nc.semaphore("s_spZ"))
        s_spL = e(nc.semaphore("s_spL"))
        s_vr = e(nc.semaphore("s_vr"))
        s_out = e(nc.semaphore("s_out"))
        stp = [e(nc.semaphore("st0")), e(nc.semaphore("st1"))]
        blk = e(nc.Block())

        NLD = 16 + 16 + 1 + 1 + 1

        @blk.gpsimd
        def _(g):
            for k in range(NK):
                g.dma_start(out=xb[:, k * TSL:(k + 1) * TSL],
                            in_=xT[k * 128:(k + 1) * 128, :]).then_inc(ld, 16)
            for k in range(NK):
                g.dma_start(out=wx_sb[:, k * 96:(k + 1) * 96],
                            in_=wxT[k * 128:(k + 1) * 128, :]).then_inc(ld, 16)
            g.dma_start(out=wdt_sb[:], in_=wdtT[:]).then_inc(ld, 16)

        @blk.sync
        def _(sy):
            sy.dma_start(out=bdt_sb[:], in_=bdt[:]).then_inc(lds, 16)

        LD_ALL = 16 * (NK + NK + 1)

        @blk.tensor
        def _(t):
            t.wait_ge(ld, LD_ALL)
            g = 0
            for tc in range(NTC):
                cs = slice(tc * 512, (tc + 1) * 512)
                if tc > 0:
                    t.wait_ge(s_dtx, tc)
                for k in range(NK):
                    mm = t.matmul(p_xp[:, :], wx_sb[:, k * 96:(k + 1) * 96],
                                  xb[:, k * TSL + tc * 512: k * TSL + tc * 512 + 512],
                                  start=(k == 0), stop=(k == NK - 1))
                mm.then_inc(s_xp, 1)
                t.wait_ge(s_dtx, tc + 1)
                for k in range(NK):
                    if g >= 2:
                        t.wait_ge(s_spZ, g - 1)
                    b = g % 2
                    t.matmul(p_dt[:, b * 512:(b + 1) * 512],
                             wdt_sb[:, k * 128:(k + 1) * 128], dtx_b[:, cs],
                             start=True, stop=True).then_inc(s_dt, 1)
                    g += 1

        @blk.scalar
        def _(a):
            a.wait_ge(lds, 16)
            g = 0
            for tc in range(NTC):
                for k in range(NK):
                    b = g % 2
                    zsl = slice(b * 512, (b + 1) * 512)
                    a.wait_ge(s_dt, g + 1)
                    if g >= 2:
                        a.wait_ge(s_vr, g - 1)
                    a.activation(t_z[:, zsl], p_dt[:, zsl], AF.Identity,
                                 bias=bdt_sb[:, k:k + 1]).then_inc(s_spZ, 1)
                    a.drain()
                    a.activation(t_a[:], t_z[:, zsl], AF.Abs)
                    a.drain()
                    a.activation(t_e[:], t_a[:], AF.Exp, scale=-1.0)
                    a.drain()
                    a.activation(t_l[:, zsl], t_e[:], AF.Ln,
                                 bias=one_c[:]).then_inc(s_spL, 1)
                    g += 1

        @blk.vector
        def _(v):
            v.memset(one_c[:], 1.0)
            g = 0
            for tc in range(NTC):
                cs = slice(tc * 512, (tc + 1) * 512)
                v.wait_ge(s_xp, tc + 1)
                v.tensor_copy(xp_sb[:, cs], p_xp[:, :])
                v.tensor_copy(dtx_b[:, cs], p_xp[0:64, :]).then_inc(s_dtx, 1)
                for k in range(NK):
                    b = g % 2
                    zsl = slice(b * 512, (b + 1) * 512)
                    v.wait_ge(s_spZ, g + 1)
                    v.tensor_scalar(t_r[:], t_z[:, zsl], 0.0, None,
                                    ALU.max).then_inc(s_vr, 1)
                    v.wait_ge(s_spL, g + 1)
                    if g >= 2:
                        v.wait_ge(stp[g % 2], 16 * (g // 2))
                    v.drain()
                    v.tensor_add(dt_out[:, zsl], t_r[:], t_l[:, zsl]).then_inc(s_out, 1)
                    g += 1

        @blk.sync
        def _(sy):
            g = 0
            for tc in range(NTC):
                for k in range(NK):
                    b = g % 2
                    sy.wait_ge(s_out, g + 1)
                    sy.dma_start(out=dtT[k * 128:(k + 1) * 128,
                                         tc * 512:(tc + 1) * 512],
                                 in_=dt_out[:, b * 512:(b + 1) * 512]).then_inc(stp[g % 2], 16)
                    g += 1
            sy.wait_ge(s_dtx, NTC)
            sy.dma_start(out=bc[:], in_=xp_sb[64:96, :]).then_inc(stp[0], 16)
            sy.wait_ge(stp[0], 16 * (NTC * NK // 2 + 1))
            sy.wait_ge(stp[1], 16 * (NTC * NK // 2))
    return nc


def build_E():
    nc = bass.Bass("TRN2", target_bir_lowering=False)
    F16 = mybir.dt.float16
    w2T = nc.dram_tensor("w2T", [FF0, 256], F32, kind="ExternalInput").ap()
    h128 = nc.dram_tensor("h128", [128, 128], BF16, kind="ExternalInput").ap()
    idt = nc.dram_tensor("idt", [128, 128], BF16, kind="ExternalInput").ap()
    gpk = nc.dram_tensor("gpk", [1408, 256], I32, kind="ExternalInput").ap()
    gsc = nc.dram_tensor("gsc", [128, 11 * 16], F16, kind="ExternalInput").ap()
    dpk = nc.dram_tensor("dpk", [256, 704], I32, kind="ExternalInput").ap()
    dsc = nc.dram_tensor("dsc", [128, 2 * 44], F16, kind="ExternalInput").ap()
    w2pT = nc.dram_tensor("w2pT", [FF0, 256], BF16, kind="ExternalOutput").ap()
    wguT = nc.dram_tensor("wguT", [DM, 1408], BF16, kind="ExternalOutput").ap()
    wdnT = nc.dram_tensor("wdnT", [FI, 256], BF16, kind="ExternalOutput").ap()

    SC = float(2.0 ** -6.5)
    NTR = 11 * 16 + 2 * 44

    with ExitStack() as ctx:
        e = ctx.enter_context
        w2s = e(nc.sbuf_tensor("w2s", [128, 2 * 256], BF16))
        t1 = e(nc.sbuf_tensor("t1", [128, 64 * 256], F32))
        tmph = e(nc.sbuf_tensor("tmph", [128, 32 * 256], F32))
        w2o = e(nc.sbuf_tensor("w2o", [128, 64 * 256], BF16))
        h_sb = e(nc.sbuf_tensor("h_sb", [128, 128], BF16))
        idt_sb = e(nc.sbuf_tensor("idt_sb", [128, 128], BF16))
        pk_sb = e(nc.sbuf_tensor("pk_sb", [128, 11 * 256], I32))
        gsc_sb = e(nc.sbuf_tensor("gsc_sb", [128, 11 * 16], F32))
        dpk_sb = e(nc.sbuf_tensor("dpk_sb", [128, 2 * 704], I32))
        dsc_sb = e(nc.sbuf_tensor("dsc_sb", [128, 2 * 44], F32))
        m_i = e(nc.sbuf_tensor("m_i", [128, 704], I32))
        mf = e(nc.sbuf_tensor("mf", [128, 704], F32))
        r1 = e(nc.sbuf_tensor("r1", [128, 704], F32))
        r2 = e(nc.sbuf_tensor("r2", [128, 704], F32))
        uu = e(nc.sbuf_tensor("uu", [128, 704], F32))
        vv = e(nc.sbuf_tensor("vv", [128, 704], F32))
        s31 = e(nc.sbuf_tensor("s31", [128, 704], I32))
        wde = e(nc.sbuf_tensor("wde", [128, 5632], F32))
        wdb = e(nc.sbuf_tensor("wdb", [128, 5632], BF16))
        wtr = e(nc.sbuf_tensor("wtr", [128, 128], BF16))
        p_f = e(nc.psum_tensor("p_f", [128, 256], F32))
        p_tr = e(nc.psum_tensor("p_tr", [128, 128], BF16))
        ld = e(nc.semaphore("ld"))
        lds2 = e(nc.semaphore("lds2"))
        ldwp = [e(nc.semaphore("ldwA")), e(nc.semaphore("ldwB"))]
        s_t = e(nc.semaphore("s_t"))
        s_v = e(nc.semaphore("s_v"))
        s_a = e(nc.semaphore("s_a"))
        st = e(nc.semaphore("st"))
        blk = e(nc.Block())

        @blk.gpsimd
        def _(g):
            g.dma_start(out=gsc_sb[:], in_=gsc[:]).then_inc(ld, 16)
            g.dma_start(out=dsc_sb[:], in_=dsc[:]).then_inc(ld, 16)
            for c in range(64):
                if c >= 2:
                    g.wait_ge(s_t, c - 1)
                b = c % 2
                g.dma_start(out=w2s[:, b * 256:(b + 1) * 256],
                            in_=w2T[c * 128:(c + 1) * 128, :]).then_inc(ldwp[c % 2], 16)

        @blk.sync
        def _(sy):
            sy.dma_start(out=h_sb[:], in_=h128[:]).then_inc(lds2, 16)
            sy.dma_start(out=idt_sb[:], in_=idt[:]).then_inc(lds2, 16)
            sy.dma_start(out=pk_sb[:].rearrange("p (c j) -> p c j", c=11),
                         in_=gpk.rearrange("(c p) j -> p c j", p=128)).then_inc(lds2, 16)
            sy.dma_start(out=dpk_sb[:].rearrange("p (c j) -> p c j", c=2),
                         in_=dpk.rearrange("(c p) j -> p c j", p=128)).then_inc(lds2, 16)

        @blk.tensor
        def _(t):
            t.wait_ge(ld, 16 * 2)
            t.wait_ge(lds2, 16 * 4)
            for c in range(64):
                t.wait_ge(ldwp[c % 2], 16 * (c // 2 + 1))
                if c >= 1:
                    t.wait_ge(s_a, c)
                b = c % 2
                mm = t.matmul(p_f[:], h_sb[:], w2s[:, b * 256:(b + 1) * 256],
                              start=True, stop=True)
                mm.then_inc(s_t, 1)
            ntr = 0
            for c in range(11):
                t.wait_ge(s_v, 2 + c)
                for k in range(16):
                    if ntr >= 1:
                        t.wait_ge(s_a, 64 + ntr)
                    mm = t.transpose(p_tr[:], wdb[:, k * 128:(k + 1) * 128], idt_sb[:])
                    mm.then_inc(s_t, 1)
                    ntr += 1
            for c in range(2):
                t.wait_ge(s_v, 13 + c)
                for k in range(44):
                    t.wait_ge(s_a, 64 + ntr)
                    mm = t.transpose(p_tr[:], wdb[:, k * 128:(k + 1) * 128], idt_sb[:])
                    mm.then_inc(s_t, 1)
                    ntr += 1

        @blk.scalar
        def _(a):
            for c in range(64):
                a.wait_ge(s_t, c + 1)
                a.drain()
                a.activation(t1[:, c * 256:(c + 1) * 256], p_f[:],
                             AF.Identity).then_inc(s_a, 1)
            for ntr in range(NTR):
                a.wait_ge(s_t, 64 + ntr + 1)
                if ntr >= 1:
                    a.wait_ge(st, 16 * (ntr + 1))
                a.drain()
                a.activation(wtr[:], p_tr[:], AF.Identity).then_inc(s_a, 1)

        @blk.vector
        def _(v):
            v.wait_ge(s_a, 64)
            for s in range(6):
                bit = 1 << s
                lo = [c for c in range(64) if not (c & bit)]
                v.drain()
                for j, c in enumerate(lo):
                    v.tensor_copy(tmph[:, j * 256:(j + 1) * 256],
                                  t1[:, c * 256:(c + 1) * 256])
                v.drain()
                for j, c in enumerate(lo):
                    cb = c | bit
                    v.tensor_add(t1[:, c * 256:(c + 1) * 256],
                                 t1[:, c * 256:(c + 1) * 256],
                                 t1[:, cb * 256:(cb + 1) * 256])
                v.drain()
                for j, c in enumerate(lo):
                    cb = c | bit
                    v.tensor_sub(t1[:, cb * 256:(cb + 1) * 256],
                                 tmph[:, j * 256:(j + 1) * 256],
                                 t1[:, cb * 256:(cb + 1) * 256])
            v.drain()
            v.tensor_scalar_mul(w2o[:], t1[:], SC).then_inc(s_v, 1)

            def dequant(pk_ap, n_in, sc_ap, n_grp, base_ms):
                wv = wde[:, 0:n_in].rearrange("p (j e) -> p e j", e=8)
                n8 = n_in // 8
                for p in range(8):
                    v.drain()
                    v.tensor_scalar(m_i[:, 0:n8], pk_ap, 4 * p, 7,
                                    ALU.logical_shift_right, ALU.bitwise_and)
                    v.drain()
                    v.tensor_single_scalar(mf[:, 0:n8], m_i[:, 0:n8], 0.0, ALU.add)
                    v.drain()
                    v.tensor_scalar(r1[:, 0:n8], mf[:, 0:n8], -4.0, 0.0,
                                    ALU.add, ALU.max)
                    v.tensor_scalar(s31[:, 0:n8], pk_ap, 28 - 4 * p,
                                    -2147483648, ALU.logical_shift_left,
                                    ALU.bitwise_and)
                    v.tensor_scalar(r2[:, 0:n8], mf[:, 0:n8], -6.0, 0.0,
                                    ALU.add, ALU.max)
                    v.drain()
                    v.tensor_add(uu[:, 0:n8], mf[:, 0:n8], r1[:, 0:n8])
                    v.drain()
                    v.scalar_tensor_tensor(vv[:, 0:n8], uu[:, 0:n8], 0.5,
                                           r2[:, 0:n8], ALU.mult, ALU.add)
                    v.drain()
                    v.tensor_tensor(wv[:, p, :].bitcast(I32),
                                    vv[:, 0:n8].bitcast(I32),
                                    s31[:, 0:n8], ALU.bitwise_xor)
                v.drain()
                for grp in range(n_grp):
                    v.tensor_scalar_mul(wdb[:, grp * 128:(grp + 1) * 128],
                                        wde[:, grp * 128:(grp + 1) * 128],
                                        sc_ap[:, grp:grp + 1])
                v.drain()
                v.engine_nop().then_inc(s_v, 1)

            for c in range(11):
                if c >= 1:
                    v.wait_ge(s_t, 64 + 16 * c)
                dequant(pk_sb[:, c * 256:(c + 1) * 256], 2048,
                        gsc_sb[:, c * 16:(c + 1) * 16], 16, None)
            for c in range(2):
                v.wait_ge(s_t, 64 + 176 + 44 * c)
                dequant(dpk_sb[:, c * 704:(c + 1) * 704], 5632,
                        dsc_sb[:, c * 44:(c + 1) * 44], 44, None)

        @blk.sync
        def _(sy):
            sy.wait_ge(s_v, 1)
            sy.dma_start(out=w2pT.rearrange("(c p) d -> p c d", p=128),
                         in_=w2o[:].rearrange("p (c d) -> p c d", c=64)).then_inc(st, 16)
            ntr = 0
            for c in range(11):
                for k in range(16):
                    sy.wait_ge(s_a, 64 + ntr + 1)
                    sy.dma_start(out=wguT[k * 128:(k + 1) * 128,
                                          c * 128:(c + 1) * 128],
                                 in_=wtr[:]).then_inc(st, 16)
                    ntr += 1
            for c in range(2):
                for k in range(44):
                    sy.wait_ge(s_a, 64 + ntr + 1)
                    sy.dma_start(out=wdnT[k * 128:(k + 1) * 128,
                                          c * 128:(c + 1) * 128],
                                 in_=wtr[:]).then_inc(st, 16)
                    ntr += 1
            sy.wait_ge(st, 16 * (1 + NTR))
    return nc


def build_C():
    nc = bass.Bass("TRN2", target_bir_lowering=False)
    xa = nc.dram_tensor("xa", [TSL, DM], F32, kind="ExternalInput").ap()
    ya = nc.dram_tensor("ya", [TSL, DM], F32, kind="ExternalInput").ap()
    w1T = nc.dram_tensor("w1T", [DM, FF0], F32, kind="ExternalInput").ap()
    w3T = nc.dram_tensor("w3T", [DM, FF0], F32, kind="ExternalInput").ap()
    w2pT = nc.dram_tensor("w2pT", [FF0, DM], BF16, kind="ExternalInput").ap()
    idt = nc.dram_tensor("idt", [128, 128], BF16, kind="ExternalInput").ap()
    z2o = nc.dram_tensor("z2o", [TSL, DM], F32, kind="ExternalOutput").ap()
    z1f = nc.dram_tensor("z1f", [TSL, DM], F32, kind="Internal").ap()
    hidT = nc.dram_tensor("hidT", [FF0, TSL], BF16, kind="Internal").ap()

    NT = TSL // 128
    NK = DM // 128
    NH0 = FF0 // 128
    INV = 1.0 / DM

    with ExitStack() as ctx:
        e = ctx.enter_context
        zt = e(nc.sbuf_tensor("zt", [128, DM], F32))
        yt_in = e(nc.sbuf_tensor("yt_in", [128, DM], F32))
        z1t = e(nc.sbuf_tensor("z1t", [128, DM], F32))
        z1b = e(nc.sbuf_tensor("z1b", [128, DM], BF16))
        mcol = e(nc.sbuf_tensor("mcol", [128, 4], F32))
        eps_c = e(nc.sbuf_tensor("eps_c", [128, 1], F32))
        idt_sb = e(nc.sbuf_tensor("idt_sb", [128, 128], BF16))
        z1T = e(nc.sbuf_tensor("z1T", [128, NK * TSL], BF16))
        w1_sb = e(nc.sbuf_tensor("w1_sb", [128, 2 * NK * 128], BF16))
        w3_sb = e(nc.sbuf_tensor("w3_sb", [128, 2 * NK * 128], BF16))
        sg = e(nc.sbuf_tensor("sg", [128, 512], BF16))
        hb = e(nc.sbuf_tensor("hb", [128, 2 * 512], BF16))
        hw_sb = e(nc.sbuf_tensor("hw_sb", [128, NH0 * 512], BF16))
        w2_sb = e(nc.sbuf_tensor("w2_sb", [128, 8 * 512], BF16))
        acc3 = [e(nc.sbuf_tensor(f"acc{i}", [128, DM], F32)) for i in range(3)]
        accs = acc3 + [yt_in]
        p_g = e(nc.psum_tensor("p_g", [128, 2 * 512], F32))
        p_u = e(nc.psum_tensor("p_u", [128, 2 * 512], F32))
        p_o = [e(nc.psum_tensor(f"p_o{i}", [128, 512], F32)) for i in range(4)]
        p_tr = p_g[:, 0:64].bitcast(BF16)
        ld = e(nc.semaphore("ld"))
        ldi = e(nc.semaphore("ldi"))
        ldwp = [e(nc.semaphore("ldwA")), e(nc.semaphore("ldwB"))]
        ldh = e(nc.semaphore("ldh"))
        ldw2p = [e(nc.semaphore(f"ldw2{i}")) for i in range(8)]
        ldz = e(nc.semaphore("ldz"))
        s_xm = e(nc.semaphore("s_xm"))
        s_sd = e(nc.semaphore("s_sd"))
        s_z1 = e(nc.semaphore("s_z1"))
        s_tr = e(nc.semaphore("s_tr"))
        s_te = e(nc.semaphore("s_te"))
        s_gu = e(nc.semaphore("s_gu"))
        s_sg = e(nc.semaphore("s_sg"))
        s_hb = e(nc.semaphore("s_hb"))
        s_w2c = e(nc.semaphore("s_w2c"))
        s_ev = e(nc.semaphore("s_ev"))
        s_ln = e(nc.semaphore("s_ln"))
        st = e(nc.semaphore("st"))
        sthp = [e(nc.semaphore("sthA")), e(nc.semaphore("sthB"))]
        st2 = e(nc.semaphore("st2"))
        blk = e(nc.Block())

        @blk.sync
        def _(sy):
            sy.dma_start(out=idt_sb[:], in_=idt[:]).then_inc(ldi, 16)
            for tt in range(NT):
                if tt >= 1:
                    sy.wait_ge(s_z1, tt)
                sy.dma_start(out=zt[:],
                             in_=xa[tt * 128:(tt + 1) * 128, :]).then_inc(ld, 16)
                sy.dma_start(out=yt_in[:],
                             in_=ya[tt * 128:(tt + 1) * 128, :]).then_inc(ld, 16)
                sy.wait_ge(s_z1, tt + 1)
                sy.dma_start(out=z1f[tt * 128:(tt + 1) * 128, :],
                             in_=z1t[:]).then_inc(st, 16)

        @blk.vector
        def _(v):
            v.memset(eps_c[:], 1e-5)
            for tt in range(NT):
                v.wait_ge(ld, 16 * (2 * tt + 2))
                v.tensor_add(zt[:], zt[:], yt_in[:])
                v.drain()
                v.tensor_reduce(mcol[:, 0:1], zt[:], AX.X, ALU.add)
                v.drain()
                v.tensor_scalar_mul(mcol[:, 0:1], mcol[:, 0:1], INV)
                v.drain()
                v.tensor_scalar(zt[:], zt[:], mcol[:, 0:1], None,
                                ALU.subtract).then_inc(s_xm, 1)
                v.wait_ge(s_sd, tt + 1)
                v.reciprocal(mcol[:, 2:3], mcol[:, 1:2])
                if tt >= 1:
                    v.wait_ge(st, 16 * tt)
                    v.wait_ge(s_tr, NK * tt)
                v.drain()
                v.tensor_scalar_mul(z1t[:], zt[:], mcol[:, 2:3])
                v.drain()
                v.tensor_copy(z1b[:], z1t[:]).then_inc(s_z1, 1)

        @blk.scalar
        def _(a):
            n = 0
            for tt in range(NT):
                a.wait_ge(s_xm, tt + 1)
                if tt >= 1:
                    a.wait_ge(st, 16 * tt)
                a.drain()
                a.activation(z1t[:], zt[:], AF.Square, accum_out=mcol[:, 3:4])
                a.drain()
                a.activation(mcol[:, 1:2], mcol[:, 3:4], AF.Sqrt, scale=INV,
                             bias=eps_c[:]).then_inc(s_sd, 1)
                for k in range(NK):
                    a.wait_ge(s_tr, n + 1)
                    a.drain()
                    a.activation(z1T[:, k * TSL + tt * 128:k * TSL + (tt + 1) * 128],
                                 p_tr, AF.Identity).then_inc(s_te, 1)
                    n += 1

        @blk.tensor
        def _(t):
            t.wait_ge(ldi, 16)
            n = 0
            for tt in range(NT):
                t.wait_ge(s_z1, tt + 1)
                for k in range(NK):
                    if n >= 1:
                        t.wait_ge(s_te, n)
                    mm = t.transpose(p_tr, z1b[:, k * 128:(k + 1) * 128],
                                     idt_sb[:])
                    mm.then_inc(s_tr, 1)
                    n += 1
            t.wait_ge(s_te, NT * NK)
            gi = 0
            for hc in range(NH0):
                t.wait_ge(ldwp[hc % 2], 32 * (hc // 2 + 1))
                hb_ = (hc % 2) * NK * 128
                for tcx in range(4):
                    gb = (gi % 2) * 512
                    if gi >= 2:
                        t.wait_ge(s_hb, gi - 1)
                    for k in range(NK):
                        t.matmul(p_g[:, gb:gb + 512],
                                 w1_sb[:, hb_ + k * 128:hb_ + (k + 1) * 128],
                                 z1T[:, k * TSL + tcx * 512:k * TSL + tcx * 512 + 512],
                                 start=(k == 0), stop=(k == NK - 1))
                    for k in range(NK):
                        mm = t.matmul(p_u[:, gb:gb + 512],
                                      w3_sb[:, hb_ + k * 128:hb_ + (k + 1) * 128],
                                      z1T[:, k * TSL + tcx * 512:k * TSL + tcx * 512 + 512],
                                      start=(k == 0), stop=(k == NK - 1))
                    mm.then_inc(s_gu, 1)
                    gi += 1
            for w in range(4):
                t.wait_ge(ldh, 16 * NH0 * (w + 1))
                for dc in range(4):
                    idx = w * 4 + dc
                    if idx > 0:
                        t.wait_ge(s_ev, idx)
                    t.wait_ge(ldz, 16 * (w + 1) * 4)
                    for hc in range(NH0):
                        widx = idx * NH0 + hc
                        t.wait_ge(ldw2p[widx % 8], 16 * (widx // 8 + 1))
                        wb = (widx % 8) * 512
                        for ttl in range(4):
                            mm = t.matmul(p_o[ttl][:],
                                          hw_sb[:, hc * 512 + ttl * 128:hc * 512 + (ttl + 1) * 128],
                                          w2_sb[:, wb:wb + 512],
                                          start=(hc == 0), stop=(hc == NH0 - 1))
                        mm.then_inc(s_w2c, 1)

        @blk.gpsimd
        def _(g):
            for hc in range(NH0):
                if hc >= 2:
                    g.wait_ge(s_gu, 4 * (hc - 1))
                hb_ = (hc % 2) * NK * 128
                g.dma_start(out=w1_sb[:, hb_:hb_ + NK * 128].rearrange(
                                "p (k h) -> p k h", k=NK),
                            in_=w1T.rearrange("(k p) h -> p k h", p=128)[
                                :, :, hc * 128:(hc + 1) * 128]).then_inc(ldwp[hc % 2], 16)
                g.dma_start(out=w3_sb[:, hb_:hb_ + NK * 128].rearrange(
                                "p (k h) -> p k h", k=NK),
                            in_=w3T.rearrange("(k p) h -> p k h", p=128)[
                                :, :, hc * 128:(hc + 1) * 128]).then_inc(ldwp[hc % 2], 16)
            for w in range(4):
                for dc in range(4):
                    idx = w * 4 + dc
                    for hc in range(NH0):
                        widx = idx * NH0 + hc
                        if widx >= 8:
                            g.wait_ge(s_w2c, widx - 7)
                        wb = (widx % 8) * 512
                        g.dma_start(out=w2_sb[:, wb:wb + 512],
                                    in_=w2pT[hc * 128:(hc + 1) * 128,
                                             dc * 512:(dc + 1) * 512]).then_inc(ldw2p[widx % 8], 16)

        @blk.scalar
        def _(a):
            gi = 0
            for hc in range(NH0):
                for tcx in range(4):
                    a.wait_ge(s_gu, gi + 1)
                    if gi >= 1:
                        a.wait_ge(s_hb, gi)
                    a.activation(sg[:], p_g[:, (gi % 2) * 512:(gi % 2) * 512 + 512],
                                 AF.Silu).then_inc(s_sg, 1)
                    gi += 1
            for li in range(NT):
                a.wait_ge(s_xm, NT + li + 1)
                if li >= 1:
                    a.wait_ge(st2, 16 * li)
                a.drain()
                a.activation(z1t[:], accs[li % 4][:], AF.Square,
                             accum_out=mcol[:, 3:4])
                a.drain()
                a.activation(mcol[:, 1:2], mcol[:, 3:4], AF.Sqrt, scale=INV,
                             bias=eps_c[:]).then_inc(s_sd, 1)

        @blk.vector
        def _(v):
            gi = 0
            for hc in range(NH0):
                for tcx in range(4):
                    v.wait_ge(s_sg, gi + 1)
                    if gi >= 2:
                        v.wait_ge(sthp[gi % 2], 16 * (gi // 2))
                    v.tensor_mul(hb[:, (gi % 2) * 512:(gi % 2 + 1) * 512],
                                 sg[:], p_u[:, (gi % 2) * 512:(gi % 2) * 512 + 512]
                                 ).then_inc(s_hb, 1)
                    gi += 1
            for w in range(4):
                for dc in range(4):
                    idx = w * 4 + dc
                    v.wait_ge(s_w2c, (idx + 1) * NH0)
                    v.wait_ge(ldz, 16 * (w + 1) * 4)
                    v.drain()
                    for ttl in range(4):
                        la = v.tensor_add(accs[ttl][:, dc * 512:(dc + 1) * 512],
                                          accs[ttl][:, dc * 512:(dc + 1) * 512],
                                          p_o[ttl][:])
                    la.then_inc(s_ev, 1)
                    v.drain()
                for ttl in range(4):
                    li = w * 4 + ttl
                    acc = accs[ttl]
                    v.tensor_reduce(mcol[:, 0:1], acc[:], AX.X, ALU.add)
                    v.drain()
                    v.tensor_scalar_mul(mcol[:, 0:1], mcol[:, 0:1], INV)
                    v.drain()
                    v.tensor_scalar(acc[:], acc[:], mcol[:, 0:1], None,
                                    ALU.subtract).then_inc(s_xm, 1)
                    v.wait_ge(s_sd, NT + li + 1)
                    v.reciprocal(mcol[:, 2:3], mcol[:, 1:2])
                    if li >= 1:
                        v.wait_ge(st2, 16 * li)
                    v.drain()
                    v.tensor_scalar_mul(z1t[:], acc[:], mcol[:, 2:3]).then_inc(s_ln, 1)

        @blk.sync
        def _(sy):
            gi = 0
            for hc in range(NH0):
                for tcx in range(4):
                    sy.wait_ge(s_hb, gi + 1)
                    sy.dma_start(out=hidT[hc * 128:(hc + 1) * 128,
                                          tcx * 512:(tcx + 1) * 512],
                                 in_=hb[:, (gi % 2) * 512:(gi % 2 + 1) * 512]
                                 ).then_inc(sthp[gi % 2], 16)
                    gi += 1
            sy.wait_ge(sthp[0], 16 * 128)
            sy.wait_ge(sthp[1], 16 * 128)
            sy.wait_ge(st, 16 * NT)
            for w in range(4):
                for hc in range(NH0):
                    if w >= 1:
                        sy.wait_ge(s_w2c, w * 4 * NH0)
                    sy.dma_start(out=hw_sb[:, hc * 512:(hc + 1) * 512],
                                 in_=hidT[hc * 128:(hc + 1) * 128,
                                          w * 512:(w + 1) * 512]).then_inc(ldh, 16)
                for ttl in range(4):
                    if w >= 1:
                        sy.wait_ge(s_ln, w * 4)
                    sy.dma_start(out=accs[ttl][:],
                                 in_=z1f[w * 512 + ttl * 128:w * 512 + (ttl + 1) * 128,
                                         :]).then_inc(ldz, 16)
                for ttl in range(4):
                    li = w * 4 + ttl
                    sy.wait_ge(s_ln, li + 1)
                    sy.dma_start(out=z2o[w * 512 + ttl * 128:w * 512 + (ttl + 1) * 128, :],
                                 in_=z1t[:]).then_inc(st2, 16)
            sy.wait_ge(st2, 16 * NT)
    return nc


def build_B():
    nc = bass.Bass("TRN2", target_bir_lowering=False)
    dtTs = nc.dram_tensor("dtTs", [256, TOK], F32, kind="ExternalInput").ap()
    xTs = nc.dram_tensor("xTs", [256, TOK], F32, kind="ExternalInput").ap()
    Ash = nc.dram_tensor("Ash", [128, 32], F32, kind="ExternalInput").ap()
    Dsh = nc.dram_tensor("Dsh", [128, 2], F32, kind="ExternalInput").ap()
    BT = nc.dram_tensor("BT", [NST, TOK], F32, kind="ExternalInput").ap()
    CT = nc.dram_tensor("CT", [NST, TOK], F32, kind="ExternalInput").ap()
    idt = nc.dram_tensor("idt", [128, 128], BF16, kind="ExternalInput").ap()
    yTs = nc.dram_tensor("yTs", [256, TOK], F32, kind="ExternalOutput").ap()

    NTC = TOK // 512
    NITER = NTC * 2

    with ExitStack() as ctx:
        e = ctx.enter_context
        btst = e(nc.sbuf_tensor("btst", [1, 2 * NST * 512], BF16))
        ctst = e(nc.sbuf_tensor("ctst", [1, 2 * NST * 512], BF16))
        alog_sb = e(nc.sbuf_tensor("alog_sb", [128, 32], F32))
        a_sb = e(nc.sbuf_tensor("a_sb", [128, 32], F32))
        d_sb = e(nc.sbuf_tensor("d_sb", [128, 2], F32))
        ones1 = e(nc.sbuf_tensor("ones1", [1, 128], BF16))
        idt_sb = e(nc.sbuf_tensor("idt_sb", [128, 128], BF16))
        dt_t = e(nc.sbuf_tensor("dt_t", [128, 2 * 512], F32))
        x_t = e(nc.sbuf_tensor("x_t", [128, 2 * 512], F32))
        u_sb = e(nc.sbuf_tensor("u_sb", [128, 512], F32))
        dA_sb = e(nc.sbuf_tensor("dA_sb", [128, 2 * 512], F32))
        dbx_sb = e(nc.sbuf_tensor("dbx_sb", [128, 2 * 512], F32))
        h_sb = e(nc.sbuf_tensor("h_sb", [128, 2 * 512], F32))
        yt_bf = e(nc.sbuf_tensor("yt_bf", [128, 2 * 512], BF16))
        yo_sb = e(nc.sbuf_tensor("yo_sb", [128, 2 * 512], F32))
        state = e(nc.sbuf_tensor("state", [128, 32], F32))
        p_br = e(nc.psum_tensor("p_br", [128, 2 * 512], F32))
        p_cr = e(nc.psum_tensor("p_cr", [128, 3 * 512], F32))
        p_y = e(nc.psum_tensor("p_y", [128, 2 * 512], F32))
        ld = e(nc.semaphore("ld"))
        ldtp = [e(nc.semaphore("ldt0")), e(nc.semaphore("ldt1"))]
        ldbp = [e(nc.semaphore("ldb0")), e(nc.semaphore("ldb1"))]
        s_t = e(nc.semaphore("s_t"))
        s_a = e(nc.semaphore("s_a"))
        s_vb = e(nc.semaphore("s_vb"))
        s_vs = e(nc.semaphore("s_vs"))
        s_vy = e(nc.semaphore("s_vy"))
        s_ty = e(nc.semaphore("s_ty"))
        s_y = e(nc.semaphore("s_y"))
        s_ai = e(nc.semaphore("s_ai"))
        stpB = [e(nc.semaphore("stB0")), e(nc.semaphore("stB1"))]
        blk = e(nc.Block())

        @blk.sync
        def _(sy):
            sy.dma_start(out=alog_sb[:], in_=Ash[:]).then_inc(ld, 16)
            sy.dma_start(out=d_sb[:], in_=Dsh[:]).then_inc(ld, 16)
            sy.dma_start(out=idt_sb[:], in_=idt[:]).then_inc(ld, 16)

        @blk.gpsimd
        def _(g):
            for i in range(NITER):
                tc, gr = i // 2, i % 2
                if gr == 0:
                    if tc >= 2:
                        g.wait_ge(s_t, 32 * (tc - 1))
                    sb = slice((tc % 2) * NST * 512, (tc % 2 + 1) * NST * 512)
                    g.dma_start(out=btst[0:1, sb],
                                in_=BT[:, tc * 512:(tc + 1) * 512]).then_inc(ldbp[tc % 2], 16)
                    g.dma_start(out=ctst[0:1, sb],
                                in_=CT[:, tc * 512:(tc + 1) * 512]).then_inc(ldbp[tc % 2], 16)
                if i >= 2:
                    g.wait_ge(s_y, i - 1)
                b = i % 2
                g.dma_start(out=dt_t[:, b * 512:(b + 1) * 512],
                            in_=dtTs[gr * 128:(gr + 1) * 128,
                                     tc * 512:(tc + 1) * 512]).then_inc(ldtp[i % 2], 16)
                g.dma_start(out=x_t[:, b * 512:(b + 1) * 512],
                            in_=xTs[gr * 128:(gr + 1) * 128,
                                    tc * 512:(tc + 1) * 512]).then_inc(ldtp[i % 2], 16)

        @blk.tensor
        def _(t):
            t.wait_ge(s_ai, 1)
            nrep = 0
            for i in range(NITER):
                tc = i // 2
                t.wait_ge(ldbp[tc % 2], 32 * (tc // 2 + 1))
                base = (tc % 2) * NST * 512
                if i >= 2:
                    t.wait_ge(s_y, i - 1)
                for n in range(NST):
                    nb = nrep % 2
                    psl = slice(nb * 512, (nb + 1) * 512)
                    psl3 = slice((nrep % 3) * 512, (nrep % 3) * 512 + 512)
                    ssl = slice(base + n * 512, base + (n + 1) * 512)
                    if nrep >= 2:
                        t.wait_ge(s_vb, nrep - 1)
                    if nrep >= 3:
                        t.wait_ge(s_vy, nrep - 2)
                    t.matmul(p_br[:, psl], ones1[:], btst[0:1, ssl],
                             start=True, stop=True)
                    mm = t.matmul(p_cr[:, psl3], ones1[:], ctst[0:1, ssl],
                                  start=True, stop=True)
                    mm.then_inc(s_t, 1)
                    if n >= 2:
                        nn = nrep - 2
                        t.wait_ge(s_vy, nn + 1)
                        t.matmul(p_y[:, (i % 2) * 512:(i % 2 + 1) * 512],
                                 idt_sb[:], yt_bf[:, (nn % 2) * 512:(nn % 2 + 1) * 512],
                                 start=(nn % NST == 0), stop=False).then_inc(s_ty, 1)
                    nrep += 1
                for nn in range(nrep - 2, nrep):
                    t.wait_ge(s_vy, nn + 1)
                    t.matmul(p_y[:, (i % 2) * 512:(i % 2 + 1) * 512],
                             idt_sb[:], yt_bf[:, (nn % 2) * 512:(nn % 2 + 1) * 512],
                             start=False, stop=(nn % NST == NST - 1)).then_inc(s_ty, 1)

        @blk.scalar
        def _(a):
            a.wait_ge(ld, 48)
            a.activation(a_sb[:], alog_sb[:], AF.Exp).then_inc(s_a, 1)
            a.wait_ge(s_ai, 1)
            nrep = 0
            for i in range(NITER):
                for n in range(NST):
                    nb = nrep % 2
                    a.wait_ge(ldtp[i % 2], 32 * (i // 2 + 1))
                    if nrep >= 2:
                        a.wait_ge(s_vs, nrep - 1)
                    a.activation(dA_sb[:, nb * 512:(nb + 1) * 512],
                                 dt_t[:, (i % 2) * 512:(i % 2 + 1) * 512],
                                 AF.Exp,
                                 scale=a_sb[:, (i % 2) * 16 + n:(i % 2) * 16 + n + 1]
                                 ).then_inc(s_a, 1)
                    nrep += 1

        @blk.vector
        def _(v):
            v.memset(ones1[:], 1.0)
            v.memset(state[:], 0.0)
            v.wait_ge(ld, 48)
            v.wait_ge(s_a, 1)
            v.tensor_scalar_mul(a_sb[:], a_sb[:], -1.0)
            v.drain()
            v.engine_nop().then_inc(s_ai, 1)
            nrep = 0
            for i in range(NITER):
                tc, gr = i // 2, i % 2
                b = i % 2
                bsl = slice(b * 512, (b + 1) * 512)
                v.wait_ge(ldtp[i % 2], 32 * (i // 2 + 1))
                v.drain()
                v.tensor_mul(u_sb[:], dt_t[:, bsl], x_t[:, bsl])
                v.drain()

                def em_dbx(n):
                    g = nrep0 + n
                    nb = g % 2
                    if g >= 2:
                        v.wait_ge(s_vs, g - 1)
                    v.wait_ge(s_t, g + 1)
                    v.tensor_mul(dbx_sb[:, nb * 512:(nb + 1) * 512], u_sb[:],
                                 p_br[:, nb * 512:(nb + 1) * 512]).then_inc(s_vb, 1)

                def em_scan(n):
                    g = nrep0 + n
                    nb = g % 2
                    scol = gr * 16 + n
                    v.wait_ge(s_vb, g + 1)
                    v.wait_ge(s_a, g + 2)
                    if g >= 2:
                        v.wait_ge(s_vy, g - 1)
                    init = 0.0 if tc % 16 == 0 else state[:, scol:scol + 1]
                    v.tensor_tensor_scan(h_sb[:, nb * 512:(nb + 1) * 512],
                                         dA_sb[:, nb * 512:(nb + 1) * 512],
                                         dbx_sb[:, nb * 512:(nb + 1) * 512],
                                         init, ALU.mult, ALU.add).then_inc(s_vs, 1)

                def em_tail(n):
                    g = nrep0 + n
                    nb = g % 2
                    scol = gr * 16 + n
                    v.wait_ge(s_vs, g + 1)
                    v.tensor_copy(state[:, scol:scol + 1],
                                  h_sb[:, nb * 512 + 511:nb * 512 + 512])
                    if g >= 2:
                        v.wait_ge(s_ty, g - 1)
                    v.tensor_mul(yt_bf[:, nb * 512:(nb + 1) * 512],
                                 h_sb[:, nb * 512:(nb + 1) * 512],
                                 p_cr[:, (g % 3) * 512:(g % 3) * 512 + 512]).then_inc(s_vy, 1)

                nrep0 = nrep
                em_dbx(0)
                em_dbx(1)
                em_scan(0)
                for n in range(2, NST):
                    em_dbx(n)
                    em_scan(n - 1)
                    em_tail(n - 2)
                em_scan(NST - 1)
                em_tail(NST - 2)
                em_tail(NST - 1)
                nrep += NST
                v.wait_ge(s_ty, nrep)
                if i >= 2:
                    v.wait_ge(stpB[i % 2], 16 * (i // 2))
                v.drain()
                v.scalar_tensor_tensor(yo_sb[:, bsl], x_t[:, bsl],
                                       d_sb[:, gr:gr + 1],
                                       p_y[:, b * 512:(b + 1) * 512],
                                       ALU.mult, ALU.add).then_inc(s_y, 1)

        @blk.sync
        def _(sy):
            for i in range(NITER):
                tc, gr = i // 2, i % 2
                b = i % 2
                sy.wait_ge(s_y, i + 1)
                sy.dma_start(out=yTs[gr * 128:(gr + 1) * 128,
                                     tc * 512:(tc + 1) * 512],
                             in_=yo_sb[:, b * 512:(b + 1) * 512]).then_inc(stpB[i % 2], 16)
            sy.wait_ge(stpB[0], 16 * NITER // 2)
            sy.wait_ge(stpB[1], 16 * NITER // 2)
    return nc


def build_D():
    nc = bass.Bass("TRN2", target_bir_lowering=False)
    z2T = nc.dram_tensor("z2T", [DM, SEQ], F32, kind="ExternalInput").ap()
    wq = nc.dram_tensor("wq", [DM, 512], F32, kind="ExternalInput").ap()
    wk = nc.dram_tensor("wk", [DM, 512], F32, kind="ExternalInput").ap()
    wv = nc.dram_tensor("wv", [DM, 512], F32, kind="ExternalInput").ap()
    masks = nc.dram_tensor("masks", [128, 5 * 512], BF16, kind="ExternalInput").ap()
    Oc = nc.dram_tensor("Oc", [512, SEQ], BF16, kind="ExternalOutput").ap()

    NK = DM // 128
    NQC = SEQ // 512
    SCL = float(DH) ** -0.5

    with ExitStack() as ctx:
        e = ctx.enter_context
        z2w = e(nc.sbuf_tensor("z2w", [128, NK * 2048], BF16))
        wq_sb = e(nc.sbuf_tensor("wq_sb", [128, NK * 128], BF16))
        wk_sb = e(nc.sbuf_tensor("wk_sb", [128, NK * 128], BF16))
        wv_sb = e(nc.sbuf_tensor("wv_sb", [128, NK * 128], BF16))
        QT = e(nc.sbuf_tensor("QT", [128, SEQ], BF16))
        KT = e(nc.sbuf_tensor("KT", [128, SEQ], BF16))
        V = e(nc.sbuf_tensor("V", [128, SEQ], BF16))
        msk = e(nc.sbuf_tensor("msk", [128, 5 * 512], BF16))
        pU = e(nc.sbuf_tensor("pU", [128, 2 * 512], BF16))
        pM = e(nc.sbuf_tensor("pM", [128, 4 * 512], BF16))
        rs_sb = e(nc.sbuf_tensor("rs_sb", [1, 512], F32))
        rc_sb = e(nc.sbuf_tensor("rc_sb", [1, 512], F32))
        vsum = e(nc.sbuf_tensor("vsum", [128, 2 * 512], F32))
        ones_kf = e(nc.sbuf_tensor("ones_kf", [128, 1], F32))
        rep_sb = e(nc.sbuf_tensor("rep_sb", [128, 512], F32))
        Ob = e(nc.sbuf_tensor("Ob", [128, 2 * 512], BF16))
        n20 = e(nc.sbuf_tensor("n20", [128, 1], F32))
        ones_k = e(nc.sbuf_tensor("ones_k", [128, 1], BF16))
        ones_r = e(nc.sbuf_tensor("ones_r", [1, 128], F32))
        p_q = e(nc.psum_tensor("p_q", [128, 2 * 512], F32))
        p_s = e(nc.psum_tensor("p_s", [128, 2 * 512], F32))
        p_rs = e(nc.psum_tensor("p_rs", [1, 512], F32))
        p_o = e(nc.psum_tensor("p_o", [128, 512], F32))
        p_rep = e(nc.psum_tensor("p_rep", [128, 512], F32))
        ld0 = e(nc.semaphore("ld0"))
        ldz = e(nc.semaphore("ldz"))
        ldw = e(nc.semaphore("ldw"))
        s_qkv = e(nc.semaphore("s_qkv"))
        s_qe = e(nc.semaphore("s_qe"))
        s_s = e(nc.semaphore("s_s"))
        s_p = e(nc.semaphore("s_p"))
        s_pm = e(nc.semaphore("s_pm"))
        s_av = e(nc.semaphore("s_av"))
        s_vv = e(nc.semaphore("s_vv"))
        s_vs = e(nc.semaphore("s_vs"))
        s_rc = e(nc.semaphore("s_rc"))
        s_rp = e(nc.semaphore("s_rp"))
        s_oe = e(nc.semaphore("s_oe"))
        s_vc = e(nc.semaphore("s_vc"))
        stpD = [e(nc.semaphore("stD0")), e(nc.semaphore("stD1"))]
        blk = e(nc.Block())

        def sched():
            ai = 0
            diag_ai = []
            plan = []
            for h in range(4):
                for qc in range(NQC):
                    for kt in range(4 * qc + 4):
                        diag = kt >= 4 * qc
                        di = len(diag_ai) if diag else -1
                        if diag:
                            diag_ai.append(ai)
                        plan.append((h, qc, kt, diag, ai, di))
                        ai += 1
            return plan, diag_ai
        PLAN, DIAG_AI = sched()
        NAI = len(PLAN)

        @blk.sync
        def _(sy):
            sy.dma_start(out=msk[:], in_=masks[:]).then_inc(ld0, 16)
            oi = 0
            for h in range(4):
                for qc in range(NQC):
                    sy.wait_ge(s_oe, oi + 1)
                    sy.dma_start(out=Oc[h * 128:(h + 1) * 128,
                                        qc * 512:(qc + 1) * 512],
                                 in_=Ob[:, (oi % 2) * 512:(oi % 2 + 1) * 512]
                                 ).then_inc(stpD[oi % 2], 16)
                    oi += 1
            sy.wait_ge(stpD[0], 16 * 32)
            sy.wait_ge(stpD[1], 16 * 32)

        @blk.gpsimd
        def _(g):
            for h in range(4):
                if h >= 1:
                    g.wait_ge(s_oe, 16 * h)
                g.dma_start(out=wq_sb[:].rearrange("p (k j) -> p k j", k=NK),
                            in_=wq.rearrange("(k p) j -> p k j", p=128)[
                                :, :, h * 128:(h + 1) * 128]).then_inc(ldw, 16)
                g.dma_start(out=wk_sb[:].rearrange("p (k j) -> p k j", k=NK),
                            in_=wk.rearrange("(k p) j -> p k j", p=128)[
                                :, :, h * 128:(h + 1) * 128]).then_inc(ldw, 16)
                g.dma_start(out=wv_sb[:].rearrange("p (k j) -> p k j", k=NK),
                            in_=wv.rearrange("(k p) j -> p k j", p=128)[
                                :, :, h * 128:(h + 1) * 128]).then_inc(ldw, 16)
                for w4 in range(4):
                    if 4 * h + w4 >= 1:
                        g.wait_ge(s_qkv, 24 * (4 * h + w4))
                    for k in range(NK):
                        g.dma_start(out=z2w[:, k * 2048:(k + 1) * 2048],
                                    in_=z2T[k * 128:(k + 1) * 128,
                                            w4 * 2048:(w4 + 1) * 2048]
                                    ).then_inc(ldz, 16)

        @blk.tensor
        def _(t):
            t.wait_ge(ld0, 16)
            t.wait_ge(s_vc, 1)
            qi = 0
            for h in range(4):
                t.wait_ge(ldw, 48 * (h + 1))
                for w4 in range(4):
                    t.wait_ge(ldz, 256 * (4 * h + w4 + 1))
                    for which, wsb in ((0, wq_sb), (1, wk_sb)):
                        for tcq in range(4):
                            if qi >= 2:
                                t.wait_ge(s_qe, qi - 1)
                            pb = (qi % 2) * 512
                            for k in range(NK):
                                mm = t.matmul(
                                    p_q[:, pb:pb + 512],
                                    wsb[:, k * 128:(k + 1) * 128],
                                    z2w[:, k * 2048 + tcq * 512:k * 2048 + tcq * 512 + 512],
                                    start=(k == 0), stop=(k == NK - 1))
                            mm.then_inc(s_qkv, 1)
                            qi += 1
                    for tt in range(16):
                        if qi >= 2:
                            t.wait_ge(s_qe, qi - 1)
                        pb = (qi % 2) * 512
                        for k in range(NK):
                            mm = t.matmul(
                                p_q[:, pb:pb + 128],
                                z2w[:, k * 2048 + tt * 128:k * 2048 + (tt + 1) * 128],
                                wv_sb[:, k * 128:(k + 1) * 128],
                                start=(k == 0), stop=(k == NK - 1))
                        mm.then_inc(s_qkv, 1)
                        qi += 1
                for qc in range(NQC):
                    gqc = h * NQC + qc
                    if gqc >= 1:
                        t.wait_ge(s_oe, gqc)
                    nkt = 4 * qc + 4
                    base = AI_BASE[(h, qc)]

                    def emit_scores(kt):
                        ai = base + kt
                        if ai >= 2:
                            t.wait_ge(s_p, ai - 1)
                        sb_ = (ai % 2) * 512
                        t.matmul(p_s[:, sb_:sb_ + 512],
                                 KT[:, kt * 128:(kt + 1) * 128],
                                 QT[:, qc * 512:(qc + 1) * 512],
                                 start=True, stop=True).then_inc(s_s, 1)

                    def emit_av(kt):
                        ai = base + kt
                        if kt >= 4 * qc:
                            di = DI_OF[ai]
                            t.wait_ge(s_pm, di + 1)
                            rhs = pM[:, (di % 4) * 512:(di % 4 + 1) * 512]
                        else:
                            t.wait_ge(s_p, ai + 1)
                            rhs = pU[:, (ai % 2) * 512:(ai % 2 + 1) * 512]
                        mm = t.matmul(p_o[:], V[:, kt * 128:(kt + 1) * 128], rhs,
                                      start=(kt == 0), stop=(kt == nkt - 1))
                        mm.then_inc(s_av, 1)

                    emit_scores(0)
                    for kt in range(1, nkt):
                        emit_scores(kt)
                        emit_av(kt - 1)
                    emit_av(nkt - 1)
                    t.wait_ge(s_vs, gqc + 1)
                    t.matmul(p_rs[:], ones_kf[:], vsum[:, 0:512],
                             start=True, stop=True).then_inc(s_s, 1)
                    t.wait_ge(s_rc, gqc + 1)
                    t.matmul(p_rep[:], ones_r[:], rc_sb[:],
                             start=True, stop=True).then_inc(s_rp, 1)

        @blk.scalar
        def _(a):
            qi = 0
            for h in range(4):
                for w4 in range(4):
                    for which in range(2):
                        for tcq in range(4):
                            a.wait_ge(s_qkv, qi + 1)
                            dst = (QT if which == 0 else KT)[
                                :, w4 * 2048 + tcq * 512:w4 * 2048 + tcq * 512 + 512]
                            pb = (qi % 2) * 512
                            if which == 0:
                                a.activation(dst, p_q[:, pb:pb + 512],
                                             AF.Identity, scale=SCL).then_inc(s_qe, 1)
                            else:
                                a.activation(dst, p_q[:, pb:pb + 512],
                                             AF.Identity).then_inc(s_qe, 1)
                            qi += 1
                    for tt in range(16):
                        a.wait_ge(s_qkv, qi + 1)
                        pb = (qi % 2) * 512
                        a.activation(V[:, (w4 * 16 + tt) * 128:(w4 * 16 + tt + 1) * 128],
                                     p_q[:, pb:pb + 128],
                                     AF.Identity).then_inc(s_qe, 1)
                        qi += 1
                for qc in range(NQC):
                    nkt = 4 * qc + 4
                    for kt in range(nkt):
                        ai = AI_BASE[(h, qc)] + kt + (h * NQC + qc)
                        ai0 = AI_BASE[(h, qc)] + kt
                        a.wait_ge(s_s, ai + 1)
                        if ai0 >= 2:
                            a.wait_ge(s_av, ai0 - 1)
                            a.wait_ge(s_vv, ai0 - 1)
                        sb_ = (ai0 % 2) * 512
                        a.activation(pU[:, (ai0 % 2) * 512:(ai0 % 2 + 1) * 512],
                                     p_s[:, sb_:sb_ + 512], AF.Exp,
                                     bias=n20[:]).then_inc(s_p, 1)

        @blk.vector
        def _(v):
            v.memset(n20[:], -20.0)
            v.memset(ones_k[:], 1.0)
            v.memset(ones_kf[:], 1.0)
            v.memset(ones_r[:], 1.0).then_inc(s_vc, 1)
            v.wait_ge(ld0, 16)
            for h in range(4):
                for qc in range(NQC):
                    gqc = h * NQC + qc
                    nkt = 4 * qc + 4
                    for kt in range(nkt):
                        ai = AI_BASE[(h, qc)] + kt
                        sp_ai = ai + gqc
                        diag = kt >= 4 * qc
                        v.wait_ge(s_p, ai + 1)
                        if diag:
                            di = DI_OF[ai]
                            r = kt - 4 * qc
                            mc = 4 if (qc == 0 and kt == 0) else r
                            if di >= 4:
                                v.wait_ge(s_av, DIAG_AI[di - 4] + 1)
                            v.tensor_mul(pM[:, (di % 4) * 512:(di % 4 + 1) * 512],
                                         pU[:, (ai % 2) * 512:(ai % 2 + 1) * 512],
                                         msk[:, mc * 512:(mc + 1) * 512]).then_inc(s_pm, 1)
                            v.drain()
                            src_ap = pM[:, (di % 4) * 512:(di % 4 + 1) * 512]
                        else:
                            src_ap = pU[:, (ai % 2) * 512:(ai % 2 + 1) * 512]
                        half = slice((kt % 2) * 512, (kt % 2) * 512 + 512)
                        if kt < 2:
                            if gqc >= 1 and kt == 0:
                                v.wait_ge(s_s, AI_BASE[(h, qc)] + gqc)
                            v.tensor_copy(vsum[:, half], src_ap).then_inc(s_vv, 1)
                        else:
                            v.wait_ge(s_vv, AI_BASE[(h, qc)] + kt - 1)
                            v.tensor_add(vsum[:, half], vsum[:, half],
                                         src_ap).then_inc(s_vv, 1)
                        if kt == nkt - 1:
                            v.drain()
                            v.tensor_add(vsum[:, 0:512], vsum[:, 0:512],
                                         vsum[:, 512:1024])
                            v.drain()
                            v.engine_nop().then_inc(s_vs, 1)
                    v.wait_ge(s_s, AI_BASE[(h, qc)] + nkt + gqc + 1)
                    v.drain()
                    v.tensor_copy(rs_sb[:], p_rs[:])
                    v.drain()
                    v.reciprocal(rc_sb[:], rs_sb[:]).then_inc(s_rc, 1)
                    v.wait_ge(s_rp, gqc + 1)
                    v.drain()
                    v.tensor_copy(rep_sb[:], p_rep[:])
                    ai_last = AI_BASE[(h, qc)] + nkt - 1
                    v.wait_ge(s_av, ai_last + 1)
                    if gqc >= 2:
                        v.wait_ge(stpD[gqc % 2], 16 * (gqc // 2))
                    v.drain()
                    v.tensor_mul(Ob[:, (gqc % 2) * 512:(gqc % 2 + 1) * 512],
                                 p_o[:], rep_sb[:]).then_inc(s_oe, 1)
    return nc


AI_BASE = {}
DI_OF = {}
def _init_attn_sched():
    ai = 0
    di = 0
    for h in range(4):
        for qc in range(16):
            AI_BASE[(h, qc)] = ai
            for kt in range(4 * qc + 4):
                if kt >= 4 * qc:
                    DI_OF[ai] = di
                    di += 1
                ai += 1
_init_attn_sched()


def build_F():
    nc = bass.Bass("TRN2", target_bir_lowering=False)
    OTs = nc.dram_tensor("OTs", [DM, TSL], BF16, kind="ExternalInput").ap()
    z2s = nc.dram_tensor("z2s", [TSL, DM], F32, kind="ExternalInput").ap()
    wout = nc.dram_tensor("wout", [DM, DM], F32, kind="ExternalInput").ap()
    boutr = nc.dram_tensor("boutr", [128, DM], F32, kind="ExternalInput").ap()
    wguT = nc.dram_tensor("wguT", [DM, 2 * FI], BF16, kind="ExternalInput").ap()
    wdnT = nc.dram_tensor("wdnT", [FI, DM], BF16, kind="ExternalInput").ap()
    idt = nc.dram_tensor("idt", [128, 128], BF16, kind="ExternalInput").ap()
    outp = nc.dram_tensor("outp", [TSL, DM], F32, kind="ExternalOutput").ap()
    x3f = nc.dram_tensor("x3f", [TSL, DM], F32, kind="Internal").ap()
    x3Tf = nc.dram_tensor("x3Tf", [DM, TSL], BF16, kind="Internal").ap()

    NT = TSL // 128
    NK = DM // 128
    NPC = FI // 128
    INV = 1.0 / DM
    NWN = 4

    with ExitStack() as ctx:
        e = ctx.enter_context
        big = e(nc.sbuf_tensor("big", [128, NK * DM], BF16))
        x3w = e(nc.sbuf_tensor("x3w", [128, NK * 512], BF16))
        x3ev = e(nc.sbuf_tensor("x3ev", [128, NK * 128], BF16))
        otile = e(nc.sbuf_tensor("otile", [128, NK * 128], BF16))
        z2t = e(nc.sbuf_tensor("z2t", [128, DM], F32))
        xrd = e(nc.sbuf_tensor("xrd", [128, DM], F32))
        x3b = e(nc.sbuf_tensor("x3b", [128, DM], BF16))
        accs = [e(nc.sbuf_tensor(f"acc{i}", [128, DM], F32)) for i in range(4)]
        outt = e(nc.sbuf_tensor("outt", [128, 2 * DM], F32))
        wg_sb = e(nc.sbuf_tensor("wg_sb", [128, 2 * NK * 128], BF16))
        wu_sb = e(nc.sbuf_tensor("wu_sb", [128, 2 * NK * 128], BF16))
        wd_sb = e(nc.sbuf_tensor("wd_sb", [128, 8 * 512], BF16))
        sg = e(nc.sbuf_tensor("sg", [128, 512], BF16))
        mcol = e(nc.sbuf_tensor("mcol", [128, 4], F32))
        eps5 = e(nc.sbuf_tensor("eps5", [128, 1], F32))
        eps6 = e(nc.sbuf_tensor("eps6", [128, 1], F32))
        idt_sb = e(nc.sbuf_tensor("idt_sb", [128, 128], BF16))
        p_op = e(nc.psum_tensor("p_op", [128, 512], F32))
        p_tr = e(nc.psum_tensor("p_tr", [128, 128], BF16))
        p_g = e(nc.psum_tensor("p_g", [128, 512], F32))
        p_u = e(nc.psum_tensor("p_u", [128, 512], F32))
        p_d = [e(nc.psum_tensor(f"p_d{i}", [128, 512], F32)) for i in range(4)]
        ld0 = e(nc.semaphore("ld0"))
        ldwt = e(nc.semaphore("ldwt"))
        ld1 = e(nc.semaphore("ld1"))
        ldgp = [e(nc.semaphore("ldgA")), e(nc.semaphore("ldgB"))]
        lddp = [e(nc.semaphore(f"ldd{i}")) for i in range(8)]
        ldx = e(nc.semaphore("ldx"))
        s_op = e(nc.semaphore("s_op"))
        s_ev1 = e(nc.semaphore("s_ev1"))
        s_e1 = e(nc.semaphore("s_e1"))
        s_xm = e(nc.semaphore("s_xm"))
        s_sd = e(nc.semaphore("s_sd"))
        s_tr = e(nc.semaphore("s_tr"))
        s_te = e(nc.semaphore("s_te"))
        s_gu = e(nc.semaphore("s_gu"))
        s_sg = e(nc.semaphore("s_sg"))
        s_hb = e(nc.semaphore("s_hb"))
        s_dc = e(nc.semaphore("s_dc"))
        s_ev2 = e(nc.semaphore("s_ev2"))
        s_rm = e(nc.semaphore("s_rm"))
        s_fin = e(nc.semaphore("s_fin"))
        st1 = e(nc.semaphore("st1"))
        stx = e(nc.semaphore("stx"))
        ldxw = e(nc.semaphore("ldxw"))
        st2p = [e(nc.semaphore("st2A")), e(nc.semaphore("st2B"))]
        blk = e(nc.Block())

        @blk.sync
        def _(sy):
            sy.dma_start(out=idt_sb[:], in_=idt[:]).then_inc(ld0, 16)
            sy.dma_start(out=xrd[:], in_=boutr[:]).then_inc(ld0, 16)
            for tt in range(NT):
                if tt >= 1:
                    sy.wait_ge(s_e1, tt)
                sy.dma_start(out=otile[:].rearrange("p (k t) -> p k t", k=NK),
                             in_=OTs.rearrange("(k p) t -> p k t", p=128)[
                                 :, :, tt * 128:(tt + 1) * 128]).then_inc(ld1, 16)
                sy.dma_start(out=z2t[:],
                             in_=z2s[tt * 128:(tt + 1) * 128, :]).then_inc(ld1, 16)
                sy.wait_ge(s_e1, tt + 1)
                sy.dma_start(out=x3f[tt * 128:(tt + 1) * 128, :],
                             in_=accs[0][:]).then_inc(st1, 16)
                sy.wait_ge(s_te, NK * (tt + 1))
                sy.dma_start(out=x3Tf.rearrange("(k p) t -> p k t", p=128)[
                                 :, :, tt * 128:(tt + 1) * 128],
                             in_=x3ev[:].rearrange("p (k t) -> p k t", k=NK)
                             ).then_inc(stx, 16)
            sy.wait_ge(st1, 16 * NT)
            sy.wait_ge(stx, 16 * NT)
            li = 0
            for wn in range(NWN):
                if wn >= 1:
                    sy.wait_ge(s_gu, wn * NPC)
                sy.dma_start(out=x3w[:].rearrange("p (k t) -> p k t", k=NK),
                             in_=x3Tf.rearrange("(k p) t -> p k t", p=128)[
                                 :, :, wn * 512:(wn + 1) * 512]).then_inc(ldxw, 16)
                for ttl in range(4):
                    if li >= 1:
                        sy.wait_ge(s_xm, NT + li)
                    sy.dma_start(out=xrd[:],
                                 in_=x3f[wn * 512 + ttl * 128:wn * 512 + (ttl + 1) * 128, :]
                                 ).then_inc(ldx, 16)
                    sy.wait_ge(s_fin, li + 1)
                    sy.dma_start(out=outp[wn * 512 + ttl * 128:wn * 512 + (ttl + 1) * 128, :],
                                 in_=outt[:, (li % 2) * DM:(li % 2 + 1) * DM]
                                 ).then_inc(st2p[li % 2], 16)
                    li += 1
            sy.wait_ge(st2p[0], 16 * NT // 2)
            sy.wait_ge(st2p[1], 16 * NT // 2)

        @blk.gpsimd
        def _(g):
            g.dma_start(out=big[:].rearrange("p (k d) -> p k d", k=NK),
                        in_=wout.rearrange("(k p) d -> p k d", p=128)).then_inc(ldwt, 16)
            for wn in range(NWN):
                for pc in range(NPC):
                    gi = wn * NPC + pc
                    if gi >= 2:
                        g.wait_ge(s_gu, gi - 1)
                    bb = (gi % 2) * NK * 128
                    g.dma_start(out=wg_sb[:, bb:bb + NK * 128].rearrange(
                                    "p (k j) -> p k j", k=NK),
                                in_=wguT.rearrange("(k p) o -> p k o", p=128)[
                                    :, :, pc * 128:(pc + 1) * 128]).then_inc(ldgp[gi % 2], 16)
                    g.dma_start(out=wu_sb[:, bb:bb + NK * 128].rearrange(
                                    "p (k j) -> p k j", k=NK),
                                in_=wguT.rearrange("(k p) o -> p k o", p=128)[
                                    :, :, FI + pc * 128:FI + (pc + 1) * 128]).then_inc(ldgp[gi % 2], 16)
                for dc in range(4):
                    for ic in range(NPC):
                        widx = (wn * 4 + dc) * NPC + ic
                        if widx >= 8:
                            g.wait_ge(s_dc, widx - 7)
                        wb = (widx % 8) * 512
                        g.dma_start(out=wd_sb[:, wb:wb + 512],
                                    in_=wdnT[ic * 128:(ic + 1) * 128,
                                             dc * 512:(dc + 1) * 512]).then_inc(lddp[widx % 8], 16)

        @blk.tensor
        def _(t):
            t.wait_ge(ld0, 32)
            t.wait_ge(ldwt, 16)
            n = 0
            for tt in range(NT):
                t.wait_ge(ld1, 32 * (tt + 1))
                for dc in range(4):
                    if tt + dc >= 1:
                        t.wait_ge(s_ev1, 4 * tt + dc)
                    for k in range(NK):
                        mm = t.matmul(p_op[:],
                                      otile[:, k * 128:(k + 1) * 128],
                                      big[:, k * DM + dc * 512:k * DM + dc * 512 + 512],
                                      start=(k == 0), stop=(k == NK - 1))
                    mm.then_inc(s_op, 1)
                t.wait_ge(s_e1, tt + 1)
                for k in range(NK):
                    if n >= 1:
                        t.wait_ge(s_te, n)
                    mm = t.transpose(p_tr[:], x3b[:, k * 128:(k + 1) * 128],
                                     idt_sb[:])
                    mm.then_inc(s_tr, 1)
                    n += 1
            for wn in range(NWN):
                t.wait_ge(ldxw, 16 * (wn + 1))
                for pc in range(NPC):
                    gi = wn * NPC + pc
                    t.wait_ge(ldgp[gi % 2], 32 * (gi // 2 + 1))
                    if gi >= 1:
                        t.wait_ge(s_hb, gi)
                    bb = (gi % 2) * NK * 128
                    for k in range(NK):
                        t.matmul(p_g[:], wg_sb[:, bb + k * 128:bb + (k + 1) * 128],
                                 x3w[:, k * 512:(k + 1) * 512],
                                 start=(k == 0), stop=(k == NK - 1))
                    for k in range(NK):
                        mm = t.matmul(p_u[:], wu_sb[:, bb + k * 128:bb + (k + 1) * 128],
                                      x3w[:, k * 512:(k + 1) * 512],
                                      start=(k == 0), stop=(k == NK - 1))
                    mm.then_inc(s_gu, 1)
                for dc in range(4):
                    wdi = wn * 4 + dc
                    if wdi >= 1:
                        t.wait_ge(s_ev2, wdi)
                    for ic in range(NPC):
                        widx = wdi * NPC + ic
                        t.wait_ge(lddp[widx % 8], 16 * (widx // 8 + 1))
                        t.wait_ge(s_hb, wn * NPC + ic + 1)
                        wb = (widx % 8) * 512
                        for ttl in range(4):
                            mm = t.matmul(p_d[ttl][:],
                                          big[:, ic * 512 + ttl * 128:ic * 512 + (ttl + 1) * 128],
                                          wd_sb[:, wb:wb + 512],
                                          start=(ic == 0), stop=(ic == NPC - 1))
                        mm.then_inc(s_dc, 1)

        @blk.scalar
        def _(a):
            n = 0
            for tt in range(NT):
                a.wait_ge(s_xm, tt + 1)
                if tt >= 1:
                    a.wait_ge(st1, 16 * tt)
                a.drain()
                a.activation(z2t[:], accs[0][:], AF.Square, accum_out=mcol[:, 3:4])
                a.drain()
                a.activation(mcol[:, 1:2], mcol[:, 3:4], AF.Sqrt, scale=INV,
                             bias=eps5[:]).then_inc(s_sd, 1)
                if tt >= 1:
                    a.wait_ge(stx, 16 * tt)
                for k in range(NK):
                    a.wait_ge(s_tr, n + 1)
                    a.drain()
                    a.activation(x3ev[:, k * 128:(k + 1) * 128],
                                 p_tr[:], AF.Identity).then_inc(s_te, 1)
                    n += 1
            for wn in range(NWN):
                for pc in range(NPC):
                    gi = wn * NPC + pc
                    a.wait_ge(s_gu, gi + 1)
                    if gi >= 1:
                        a.wait_ge(s_hb, gi)
                    a.drain()
                    a.activation(sg[:], p_g[:], AF.Silu).then_inc(s_sg, 1)
                for ttl in range(4):
                    li = wn * 4 + ttl
                    a.wait_ge(s_ev2, 4 * (wn + 1))
                    a.wait_ge(s_fin, li)
                    a.drain()
                    a.activation(z2t[:], accs[ttl][:], AF.Square,
                                 accum_out=mcol[:, 3:4])
                    a.drain()
                    a.activation(mcol[:, 1:2], mcol[:, 3:4], AF.Sqrt, scale=1.0 / DM,
                                 bias=eps6[:]).then_inc(s_rm, 1)
                    a.wait_ge(s_xm, NT + li + 1)
                    a.drain()
                    a.activation(z2t[:], accs[ttl][:], AF.Square,
                                 accum_out=mcol[:, 3:4])
                    a.drain()
                    a.activation(mcol[:, 1:2], mcol[:, 3:4], AF.Sqrt, scale=INV,
                                 bias=eps5[:]).then_inc(s_sd, 1)

        @blk.vector
        def _(v):
            v.memset(eps5[:], 1e-5)
            v.memset(eps6[:], 1e-6)
            for tt in range(NT):
                acc = accs[0]
                if tt >= 1:
                    v.wait_ge(st1, 16 * tt)
                for dc in range(4):
                    v.wait_ge(s_op, 4 * tt + dc + 1)
                    v.drain()
                    v.tensor_add(acc[:, dc * 512:(dc + 1) * 512], p_op[:],
                                 xrd[:, dc * 512:(dc + 1) * 512]).then_inc(s_ev1, 1)
                v.wait_ge(ld1, 32 * (tt + 1))
                v.drain()
                v.tensor_add(acc[:], acc[:], z2t[:])
                v.drain()
                v.tensor_reduce(mcol[:, 0:1], acc[:], AX.X, ALU.add)
                v.drain()
                v.tensor_scalar_mul(mcol[:, 0:1], mcol[:, 0:1], INV)
                v.drain()
                v.tensor_scalar(acc[:], acc[:], mcol[:, 0:1], None,
                                ALU.subtract).then_inc(s_xm, 1)
                v.wait_ge(s_sd, tt + 1)
                v.reciprocal(mcol[:, 2:3], mcol[:, 1:2])
                if tt >= 1:
                    v.wait_ge(s_tr, NK * tt)
                v.drain()
                v.tensor_scalar_mul(acc[:], acc[:], mcol[:, 2:3])
                v.drain()
                v.tensor_copy(x3b[:], acc[:]).then_inc(s_e1, 1)
            for wn in range(NWN):
                for pc in range(NPC):
                    gi = wn * NPC + pc
                    v.wait_ge(s_sg, gi + 1)
                    if wn >= 1:
                        pass
                    if wn >= 1 and pc == 0:
                        v.wait_ge(s_dc, wn * 4 * NPC)
                    v.drain()
                    v.tensor_mul(big[:, pc * 512:(pc + 1) * 512], sg[:], p_u[:]
                                 ).then_inc(s_hb, 1)
                for dc in range(4):
                    wdi = wn * 4 + dc
                    v.wait_ge(s_dc, (wdi + 1) * NPC)
                    v.drain()
                    for ttl in range(4):
                        la = v.tensor_copy(accs[ttl][:, dc * 512:(dc + 1) * 512],
                                           p_d[ttl][:])
                    la.then_inc(s_ev2, 1)
                    v.drain()
                for ttl in range(4):
                    li = wn * 4 + ttl
                    acc = accs[ttl]
                    v.wait_ge(s_rm, li + 1)
                    v.reciprocal(mcol[:, 2:3], mcol[:, 1:2])
                    v.drain()
                    v.tensor_scalar_mul(acc[:], acc[:], mcol[:, 2:3])
                    v.wait_ge(ldx, 16 * (li + 1))
                    v.drain()
                    v.tensor_add(acc[:], acc[:], xrd[:])
                    v.drain()
                    v.tensor_reduce(mcol[:, 0:1], acc[:], AX.X, ALU.add)
                    v.drain()
                    v.tensor_scalar_mul(mcol[:, 0:1], mcol[:, 0:1], INV)
                    v.drain()
                    v.tensor_scalar(acc[:], acc[:], mcol[:, 0:1], None,
                                    ALU.subtract).then_inc(s_xm, 1)
                    v.wait_ge(s_sd, NT + li + 1)
                    v.reciprocal(mcol[:, 2:3], mcol[:, 1:2])
                    if li >= 2:
                        v.wait_ge(st2p[li % 2], 16 * (li // 2))
                    v.drain()
                    v.tensor_scalar_mul(outt[:, (li % 2) * DM:(li % 2 + 1) * DM],
                                        acc[:], mcol[:, 2:3]).then_inc(s_fin, 1)
    return nc


_BUILT = {}


def _get(name, fn):
    if name not in _BUILT:
        _BUILT[name] = fn()
    return _BUILT[name]


def _hadamard128():
    H = np.array([[1.0]], dtype=np.float32)
    while H.shape[0] < 128:
        H = np.block([[H, H], [H, -H]])
    return H


def kernel(**ins):
    x = np.ascontiguousarray(ins["L0_attn_x"].reshape(TOK, DM))
    xT = np.ascontiguousarray(x.T)
    wxT = np.ascontiguousarray(ins["L0_attn_W_x"].T)
    wdtT = np.ascontiguousarray(ins["L0_attn_W_dt"].T)
    bdt = np.ascontiguousarray(ins["L0_attn_b_dt"].reshape(16, 128).T)
    idt = np.eye(128, dtype=BF)
    h128 = np.ascontiguousarray(_hadamard128().astype(BF))
    cores = list(range(NCORE))

    W2T = np.ascontiguousarray(ins["L0_ffn_W2"].T)
    gup = ins["L1_ffn_gate_up_packed"]
    gscT = np.ascontiguousarray(ins["L1_ffn_gate_up_scales"].T)
    dpk = ins["L1_ffn_down_packed"]
    dscT = np.ascontiguousarray(ins["L1_ffn_down_scales"].T)
    ncE = _get("E", build_E)
    in_maps = []
    for c in range(NCORE):
        dsl = slice(c * 256, (c + 1) * 256)
        gpk_c = np.ascontiguousarray(np.concatenate(
            [gup[c * 704:(c + 1) * 704],
             gup[FI + c * 704:FI + (c + 1) * 704]], axis=0))
        gsc_c = np.ascontiguousarray(np.concatenate(
            [gscT[c * 704:(c + 1) * 704],
             gscT[FI + c * 704:FI + (c + 1) * 704]], axis=0)
            .reshape(11, 128, 16).transpose(1, 0, 2).reshape(128, 176))
        dpk_c = np.ascontiguousarray(dpk[c * 256:(c + 1) * 256])
        dsc_c = np.ascontiguousarray(dscT[c * 256:(c + 1) * 256]
                                     .reshape(2, 128, 44).transpose(1, 0, 2)
                                     .reshape(128, 88))
        in_maps.append(dict(w2T=np.ascontiguousarray(W2T[:, dsl]), h128=h128,
                            idt=idt, gpk=gpk_c, gsc=gsc_c, dpk=dpk_c, dsc=dsc_c))
    resE = run_bass_kernel_spmd(ncE, in_maps, core_ids=cores)
    w2pT = np.concatenate([r["w2pT"] for r in resE.results], axis=1)
    wguT = np.zeros((DM, 2 * FI), dtype=BF)
    wdnT = np.concatenate([r["wdnT"] for r in resE.results], axis=1)
    for c in range(NCORE):
        blkc = resE.results[c]["wguT"]
        wguT[:, c * 704:(c + 1) * 704] = blkc[:, 0:704]
        wguT[:, FI + c * 704:FI + (c + 1) * 704] = blkc[:, 704:1408]

    ncA = _get("A", build_A)
    in_maps = []
    for c in range(NCORE):
        in_maps.append(dict(
            xT=np.ascontiguousarray(xT[:, c * TSL:(c + 1) * TSL]),
            wxT=wxT, wdtT=wdtT, bdt=bdt))
    resA = run_bass_kernel_spmd(ncA, in_maps, core_ids=cores)
    dtT = np.concatenate([r["dtT"] for r in resA.results], axis=1)
    bcs = np.concatenate([r["bc"] for r in resA.results], axis=1)
    BT = np.ascontiguousarray(bcs[0:16])
    CT = np.ascontiguousarray(bcs[16:32])

    A_log = ins["L0_attn_A_log"]
    Dv = ins["L0_attn_D"]
    ncB = _get("B", build_B)
    in_maps = []
    for c in range(NCORE):
        rs = slice(c * 256, (c + 1) * 256)
        in_maps.append(dict(
            dtTs=np.ascontiguousarray(dtT[rs]),
            xTs=np.ascontiguousarray(xT[rs]),
            Ash=np.ascontiguousarray(
                A_log[rs].reshape(2, 128, NST).transpose(1, 0, 2).reshape(128, 32)),
            Dsh=np.ascontiguousarray(Dv[rs].reshape(2, 128).T),
            BT=BT, CT=CT, idt=idt))
    resB = run_bass_kernel_spmd(ncB, in_maps, core_ids=cores)
    yT = np.concatenate([r["yTs"] for r in resB.results], axis=0)
    y = np.ascontiguousarray(yT.T)

    w1T_h = np.ascontiguousarray(ins["L0_ffn_W1"].T)
    w3T_h = np.ascontiguousarray(ins["L0_ffn_W3"].T)
    ncC = _get("C", build_C)
    in_maps = []
    for c in range(NCORE):
        ts = slice(c * TSL, (c + 1) * TSL)
        in_maps.append(dict(xa=np.ascontiguousarray(x[ts]),
                            ya=np.ascontiguousarray(y[ts]),
                            w1T=w1T_h, w3T=w3T_h, w2pT=w2pT, idt=idt))
    resC = run_bass_kernel_spmd(ncC, in_maps, core_ids=cores)
    z2 = np.concatenate([r["z2o"] for r in resC.results], axis=0)

    wqkv = ins["L1_attn_w_qkv"]
    masks_h = _attn_masks()
    ncD = _get("D", build_D)
    in_maps = []
    for c in range(NCORE):
        beta = c // 4
        h0 = (c % 4) * 4
        z2T_b = np.ascontiguousarray(z2[beta * SEQ:(beta + 1) * SEQ].T)
        in_maps.append(dict(
            z2T=z2T_b,
            wq=np.ascontiguousarray(wqkv[:, h0 * DH:(h0 + 4) * DH]),
            wk=np.ascontiguousarray(wqkv[:, DM + h0 * DH:DM + (h0 + 4) * DH]),
            wv=np.ascontiguousarray(wqkv[:, 2 * DM + h0 * DH:2 * DM + (h0 + 4) * DH]),
            masks=masks_h))
    resD = run_bass_kernel_spmd(ncD, in_maps, core_ids=cores)
    OT = [np.concatenate([resD.results[b * 4 + j]["Oc"] for j in range(4)], axis=0)
          for b in range(2)]
    OTfull = np.concatenate(OT, axis=1)

    wout_h = np.ascontiguousarray(ins["L1_attn_w_out"])
    boutr_h = np.ascontiguousarray(np.tile(ins["L1_attn_b_out"][None, :], (128, 1)))
    ncF = _get("F", build_F)
    in_maps = []
    for c in range(NCORE):
        ts = slice(c * TSL, (c + 1) * TSL)
        in_maps.append(dict(
            OTs=np.ascontiguousarray(OTfull[:, ts]),
            z2s=np.ascontiguousarray(z2[ts]),
            wout=wout_h, boutr=boutr_h, wguT=wguT, wdnT=wdnT, idt=idt))
    resF = run_bass_kernel_spmd(ncF, in_maps, core_ids=cores)
    out = np.concatenate([r["outp"] for r in resF.results], axis=0)
    return out.reshape(2, SEQ, DM)


def _attn_masks():
    m = np.zeros((128, 5 * 512), dtype=BF)
    kk = np.arange(128)[:, None]
    qq = np.arange(512)[None, :]
    for r in range(4):
        m[:, r * 512:(r + 1) * 512] = (kk + r * 128 <= qq).astype(BF)
    corner = ((kk <= qq) | (qq < 128)).astype(BF)
    m[:, 4 * 512:5 * 512] = corner
    return m
